# version 13 (final)
# baseline (speedup 1.0000x reference)
import numpy as np
import ml_dtypes
import concourse.bass as bass
import concourse.mybir as mybir
from concourse.bass_core_rust import IndirectOffsetOnAxis
from concourse.bass_utils import run_bass_kernel_spmd

F32 = mybir.dt.float32
BF16 = mybir.dt.bfloat16
U32 = mybir.dt.uint32
AF = mybir.ActivationFunctionType
ALU = mybir.AluOpType
AX = mybir.AxisListType

B, S, D, H, HD = 8, 4096, 1024, 16, 64
NDC = 8
NG = 8
EPS_LN = 1e-5
EPS_DEN = 1e-6
RD = np.float32(1.0 / D)
SENT = 16777216.0
ENGINES = ("sync", "gpsimd", "tensor", "scalar", "vector")


def make_pe_np(seq_len=S, d_pos=32):
    import math
    pos = np.arange(seq_len, dtype=np.float32)[:, None]
    div = np.exp(np.arange(0, d_pos, 2, dtype=np.float32)
                 * np.float32(-math.log(10000.0) / d_pos))
    ang = (pos * div).astype(np.float32)
    pe = np.zeros((seq_len, d_pos), dtype=np.float32)
    pe[:, 0::2] = np.sin(ang)
    pe[:, 1::2] = np.cos(ang)
    return pe


class Ctr:
    def __init__(self, nc, name, step):
        self.h = nc.alloc_semaphore(name)
        self.n = 0
        self.step = step

    def tick(self):
        self.n += self.step
        return self.n


class Em:
    def __init__(self, nc):
        self.nc = nc
        self.steps = {e: [] for e in ENGINES}
        self.waited = {e: {} for e in ENGINES}

    def wait(self, e, ctr, val):
        if val is None or val <= 0:
            return
        if self.waited[e].get(ctr, 0) >= val:
            return
        self.waited[e][ctr] = val
        self.steps[e].append(lambda eng, h=ctr.h, v=val: eng.wait_ge(h, v))

    def op(self, e, fn, inc=None):
        if inc is None:
            self.steps[e].append(fn)
            return None
        tok = inc.tick()
        self.steps[e].append(lambda eng, f=fn, h=inc.h, s=inc.step: f(eng).then_inc(h, s))
        return tok


def build_program(NT=32, DBG=(), STOP=None, SKIP=()):
    from contextlib import ExitStack
    nc = bass.Bass("TRN2", target_bir_lowering=False)
    NTOK = NT * 128
    NRT = NT + 7
    NTS = NRT * 128

    x_in = nc.dram_tensor("x_in", [NTOK, D], F32, kind="ExternalInput").ap()
    wkT = nc.dram_tensor("wkT", [128, NDC, D], F32, kind="ExternalInput").ap()
    wvT = nc.dram_tensor("wvT", [128, NDC, D], F32, kind="ExternalInput").ap()
    wqT = nc.dram_tensor("wqT", [128, NDC, D], F32, kind="ExternalInput").ap()
    wfT = nc.dram_tensor("wfT", [8, 128, NDC, D], F32, kind="ExternalInput").ap()
    bk_row = nc.dram_tensor("bk_row", [1, D], F32, kind="ExternalInput").ap()
    bv_row = nc.dram_tensor("bv_row", [1, D], F32, kind="ExternalInput").ap()
    bq_col = nc.dram_tensor("bq_col", [128, 8], F32, kind="ExternalInput").ap()
    g_row = nc.dram_tensor("g_row", [1, D], F32, kind="ExternalInput").ap()
    be_row = nc.dram_tensor("be_row", [1, D], F32, kind="ExternalInput").ap()
    bffn = nc.dram_tensor("bffn", [8, D], F32, kind="ExternalInput").ap()
    sigsT = nc.dram_tensor("sigsT", [32, 8], F32, kind="ExternalInput").ap()
    peJ = nc.dram_tensor("peJ", [32, NT, 128], F32, kind="ExternalInput").ap()
    ltri = nc.dram_tensor("ltri", [128, 129], BF16, kind="ExternalInput").ap()
    ltri8 = nc.dram_tensor("ltri8", [8, 9], F32, kind="ExternalInput").ap()
    iota_pj = nc.dram_tensor("iota_pj", [128, NT], F32, kind="ExternalInput").ap()
    i128row = nc.dram_tensor("i128row", [1, NRT], F32, kind="ExternalInput").ap()
    thr832 = nc.dram_tensor("thr832", [8, 32], F32, kind="ExternalInput").ap()
    iota_p8 = nc.dram_tensor("iota_p8", [128, 1], F32, kind="ExternalInput").ap()
    identf = nc.dram_tensor("identf", [128, 128], F32, kind="ExternalInput").ap()
    identb = nc.dram_tensor("identb", [128, 128], BF16, kind="ExternalInput").ap()
    y_out = nc.dram_tensor("y_out", [NTOK, D], F32, kind="ExternalOutput").ap()
    g_dram = nc.dram_tensor("g_dram", [64, 128], F32, kind="Internal").ap()
    m_dram = nc.dram_tensor("m_dram", [64, 128], F32, kind="Internal").ap()
    guard = nc.dram_tensor("dbg_guard", [128, 512], F32, kind="ExternalOutput").ap() if "guard" in DBG else None
    xn_dram = nc.dram_tensor("xn_dram", [NTS, D], BF16, kind="Internal").ap()
    q_dram = nc.dram_tensor("q_dram", [8, 128, NTS], BF16, kind="Internal").ap()
    wf_bf_flat = nc.dram_tensor("wf_bf", [8 * 128 * NDC, D], BF16, kind="Internal").ap()
    wf_bf = wf_bf_flat.rearrange("(t p dc) o -> t p dc o", t=8, p=128)
    bias_bf = nc.dram_tensor("bias_bf", [8, D], BF16, kind="Internal").ap()

    dbg_out = {}
    for name, shape, dt_ in (("f", [128, NT, 8], F32), ("dest", [128, NT], F32),
                             ("g", [128, 64], F32), ("tid", [1, NRT], F32),
                             ("kv", [128, 1536], F32), ("widx", [128, NRT], F32),
                             ("k0", [128, D], BF16), ("xn0", [128, D], BF16), ("msk", [128, 64], F32),
                             ("xn20", [128, D], BF16), ("qb0", [128, 512], BF16), ("y0", [128, D], F32),
                             ("xnt0", [128, 256], BF16)):
        if name in DBG:
            dbg_out[name] = nc.dram_tensor("dbg_" + name, shape, dt_,
                                           kind="ExternalOutput").ap()

    em = Em(nc)
    dX = Ctr(nc, "dX", 16)
    dG = Ctr(nc, "dG", 16)
    dT = Ctr(nc, "dT", 16)
    dO = Ctr(nc, "dO", 16)
    dQ = Ctr(nc, "dQ", 16)
    dGW0 = Ctr(nc, "dGW0", 16)
    dGW1 = Ctr(nc, "dGW1", 16)
    dX2a = Ctr(nc, "dX2a", 16)
    dX2b = Ctr(nc, "dX2b", 16)
    dXNa = Ctr(nc, "dXNa", 16)
    dXNb = Ctr(nc, "dXNb", 16)
    dB = Ctr(nc, "dB", 16)
    dGX = Ctr(nc, "dGX", 16)
    dWT = Ctr(nc, "dWT", 16)
    pP = Ctr(nc, "pP", 1)
    pA = Ctr(nc, "pA", 1)
    pV = Ctr(nc, "pV", 1)

    es = ExitStack()
    sb = lambda name, shape, dtype=F32: es.enter_context(nc.sbuf_tensor(name, shape, dtype))[:]

    def finish():
        em.wait("sync", dX, dX.n)
        em.wait("sync", dO, dO.n)
        em.wait("sync", dQ, dQ.n)
        em.wait("gpsimd", dG, dG.n)
        em.wait("gpsimd", dGW0, dGW0.n)
        em.wait("gpsimd", dGW1, dGW1.n)
        em.wait("gpsimd", dX2a, dX2a.n)
        em.wait("gpsimd", dX2b, dX2b.n)
        em.wait("gpsimd", dB, dB.n)
        em.wait("gpsimd", dGX, dGX.n)
        em.wait("gpsimd", dWT, dWT.n)
        em.wait("scalar", dT, dT.n)
        em.wait("tensor", pP, pP.n)
        em.wait("vector", pV, pV.n)
        em.wait("scalar", pA, pA.n)
        with nc.Block() as block:
            for ename in ENGINES:
                getattr(block, ename)(
                    lambda eng, el=tuple(em.steps[ename]): [fn(eng) for fn in el])
        es.close()
        return nc

    xnT = [sb(f"xnT{dc}", [128, NTS], BF16) for dc in range(NDC)]
    kvblk = sb("kvblk", [128, NG, 130], BF16)
    gamma_b = sb("gamma_b", [128, D])
    beta_b = sb("beta_b", [128, D])
    ones1f = sb("ones1f", [1, 128])
    ones1b = sb("ones1b", [1, 128], BF16)
    bkr_sb = sb("bkr_sb", [1, D], BF16)
    ber_bf = sb("ber_bf", [1, D], BF16)
    bvr_sb = sb("bvr_sb", [1, D], BF16)
    bqc_sb = sb("bqc_sb", [128, 8])
    bffn_sb = sb("bffn_sb", [8, D], BF16)
    sigs_sb = sb("sigs_sb", [32, 8])
    sgn_sb = sb("sgn_sb", [32, 8])
    eps_sb = sb("eps_sb", [128, 1])
    g_sb = sb("g_sb", [128, 64])
    gu_sb = sb("gu_sb", [128, 64], U32)
    gsafe_sb = sb("gsafe_sb", [128, 64])
    gusafe_sb = sb("gusafe_sb", [128, 64], U32)
    msk_sb = sb("msk_sb", [128, 64])
    tid_sb = sb("tid_sb", [1, NRT])
    tidu_sb = sb("tidu_sb", [1, NRT], U32)
    widx_sb = sb("widx_sb", [128, NRT])
    widxu_sb = sb("widxu_sb", [128, NRT], U32)
    tidbu_sb = sb("tidbu_sb", [128, NRT], U32)
    iota8_sb = sb("iota8_sb", [128, 1])
    identb_sb = sb("identb_sb", [128, 128], BF16)
    kvdbg = sb("kvdbg", [128, 1536]) if "kv" in DBG else None
    T = {}
    from contextlib import ExitStack as _ES
    es0p = _ES()
    if True:
        _peJ = es0p.enter_context(nc.sbuf_tensor("peJ_sb", [32, NT * 128], F32))
        _grow = es0p.enter_context(nc.sbuf_tensor("grow_sb", [1, D], F32))
        _berow = es0p.enter_context(nc.sbuf_tensor("berow_sb", [1, D], F32))
        _ltri = es0p.enter_context(nc.sbuf_tensor("ltri_sb", [128, 129], BF16))
        _ltri8 = es0p.enter_context(nc.sbuf_tensor("ltri8_sb", [8, 9], F32))
        _iotaJ = es0p.enter_context(nc.sbuf_tensor("iotaJ_sb", [128, NT], F32))
        _i128 = es0p.enter_context(nc.sbuf_tensor("i128_sb", [1, 64], F32))
        _thr = es0p.enter_context(nc.sbuf_tensor("thr_sb", [8, 32], F32))
        _identf = es0p.enter_context(nc.sbuf_tensor("identf_sb", [128, 128], F32))
        _scores = es0p.enter_context(nc.sbuf_tensor("scores_sb", [128, NT, 8], F32))
        _m = es0p.enter_context(nc.sbuf_tensor("m_sb", [128, NT], F32))
        _eq = es0p.enter_context(nc.sbuf_tensor("eq_sb", [128, NT], F32))
        _noty = es0p.enter_context(nc.sbuf_tensor("noty_sb", [128, NT], F32))
        _f = es0p.enter_context(nc.sbuf_tensor("f_sb", [128, NT, 8], F32))
        _c8 = es0p.enter_context(nc.sbuf_tensor("c8_sb", [128, 8], F32))
        _cbf = es0p.enter_context(nc.sbuf_tensor("cbf_sb", [128, 8], BF16))
        _pt = es0p.enter_context(nc.sbuf_tensor("pt_sb", [8, 129], F32))
        _small = es0p.enter_context(nc.sbuf_tensor("small_sb", [8, 40], F32))
        _rows = es0p.enter_context(nc.sbuf_tensor("rows_sb", [1, 20], F32))
        _ptm = es0p.enter_context(nc.sbuf_tensor("ptm_sb", [128, 8], F32))
        _padb = es0p.enter_context(nc.sbuf_tensor("padb_sb", [128, 8], F32))
        _pp = es0p.enter_context(nc.sbuf_tensor("pp_sb", [128, 8], F32))
        _w = es0p.enter_context(nc.sbuf_tensor("w_sb", [128, NT, 8], F32))
        _w3 = es0p.enter_context(nc.sbuf_tensor("w3_sb", [128, NT, 8], F32))
        _dest = es0p.enter_context(nc.sbuf_tensor("dest_sb", [128, NT], F32))
        _destu = es0p.enter_context(nc.sbuf_tensor("destu_sb", [128, NT], U32))
        _sent = es0p.enter_context(nc.sbuf_tensor("sent_sb", [64, 128], F32))
        _zero = es0p.enter_context(nc.sbuf_tensor("zero_sb", [64, 128], F32))
        _onespj = es0p.enter_context(nc.sbuf_tensor("onespj_sb", [128, NT], F32))
        _tmpr = es0p.enter_context(nc.sbuf_tensor("tmpr_sb", [1, 64], F32))
        _gld = es0p.enter_context(nc.sbuf_tensor("gld_sb", [64, 128], F32))
        _mld = es0p.enter_context(nc.sbuf_tensor("mld_sb", [64, 128], F32))
        _sc_ps = es0p.enter_context(nc.psum_tensor("sc_ps", [128, NT * 8], F32))
        _pt_ps = es0p.enter_context(nc.psum_tensor("pt_ps", [8, 129], F32))
        _pad_ps = es0p.enter_context(nc.psum_tensor("pad_ps", [1, 9], F32))
        _padb_ps = es0p.enter_context(nc.psum_tensor("padb_ps", [128, 8], F32))
        _wix_ps = es0p.enter_context(nc.psum_tensor("wix_ps", [128, 64], F32))
        _tr_ps = es0p.enter_context(nc.psum_tensor("tr_ps", [128, 128], F32))
        _gb_ps = es0p.enter_context(nc.psum_tensor("gb_ps", [128, D], F32))
        peJ_sb, grow_sb, berow_sb = _peJ[:], _grow[:], _berow[:]
        ltri_s, ltri8_s, iotaJ = _ltri[:], _ltri8[:], _iotaJ[:]
        i128, thr_s, idf = _i128[:][:, 0:NRT], _thr[:], _identf[:]
        scores_sb, m_sb, eq_sb, noty_sb, f_sb = _scores[:], _m[:], _eq[:], _noty[:], _f[:]
        c8_sb, cbf_sb, pt_sb = _c8[:], _cbf[:], _pt[:]
        ntc, thrcnt, padc = _small[:][:, 0:1], _small[:][:, 2:34], _small[:][:, 36:37]
        padrow, inclrow = _rows[:][:, 0:9], _rows[:][:, 10:18]
        ptm, padb, pp_sb = _ptm[:], _padb[:], _pp[:]
        w_sb, w3_sb, dest_sb, destu = _w[:], _w3[:], _dest[:], _destu[:]
        sent_sb, zero_sb, onespj = _sent[:], _zero[:], _onespj[:]
        tmpr = _tmpr[:][:, 0:NRT]
        gld_sb, mld_sb = _gld[:], _mld[:]
        sc_ps, pt_ps, pad_ps, padb_ps, wix_ps, tr_ps, gb_ps = (
            _sc_ps[:], _pt_ps[:], _pad_ps[:], _padb_ps[:], _wix_ps[:][:, 0:NRT],
            _tr_ps[:], _gb_ps[:])

        for dst, src in ((peJ_sb, peJ.rearrange("d j p -> d (j p)")), (grow_sb, g_row),
                         (berow_sb, be_row), (ltri_s, ltri), (ltri8_s, ltri8),
                         (iotaJ, iota_pj), (_i128[:][:, 0:NRT], i128row), (thr_s, thr832),
                         (idf, identf), (bqc_sb, bq_col), (sigs_sb, sigsT),
                         (iota8_sb, iota_p8), (identb_sb, identb)):
            t_ld = em.op("sync", lambda e, d=dst, s=src: e.dma_start(out=d, in_=s), dX)
        T[("lds",)] = t_ld
        t_bffn = em.op("gpsimd", lambda e: e.dma_start(out=bffn_sb, in_=bffn), dG)
        t_bkr = em.op("gpsimd", lambda e: e.dma_start(out=bkr_sb, in_=bk_row), dG)
        T[("berbf",)] = em.op("gpsimd", lambda e: e.dma_start(out=ber_bf, in_=be_row), dG)
        t_bvr = em.op("gpsimd", lambda e: e.dma_start(out=bvr_sb, in_=bv_row), dG)
        T[("bkr",)] = t_bkr
        T[("bvr",)] = t_bvr
        em.wait("gpsimd", dG, t_bffn)
        T[("bias_bf",)] = em.op("gpsimd", lambda e: e.dma_start(out=bias_bf, in_=bffn_sb), dG)

        for ap_, val in ((ones1f, 1.0), (ones1b, 1.0), (eps_sb, EPS_LN),
                         (noty_sb, 1.0), (onespj, 1.0), (sent_sb, SENT),
                         (zero_sb, 0.0), (tid_sb, 0.0)):
            t_ms = em.op("vector", lambda e, a=ap_, v=val: e.memset(a, v), pV)
        T[("ms",)] = t_ms

        em.wait("tensor", pV, T[("ms",)])
        em.wait("tensor", dX, T[("lds",)])
        em.op("tensor", lambda e: e.matmul(gb_ps[:, 0:512], ones1f, grow_sb[:, 0:512],
                                           start=True, stop=True))
        t_g = em.op("tensor", lambda e: e.matmul(gb_ps[:, 512:1024], ones1f,
                                                 grow_sb[:, 512:1024], start=True, stop=True), pP)
        em.wait("vector", pP, t_g)
        t_gev = em.op("vector", lambda e: e.tensor_copy(gamma_b, gb_ps), pV)
        em.wait("tensor", pV, t_gev)
        em.op("tensor", lambda e: e.matmul(gb_ps[:, 0:512], ones1f, berow_sb[:, 0:512],
                                           start=True, stop=True))
        t_b = em.op("tensor", lambda e: e.matmul(gb_ps[:, 512:1024], ones1f,
                                                 berow_sb[:, 512:1024], start=True, stop=True), pP)
        em.wait("vector", pP, t_b)
        T[("gb",)] = em.op("vector", lambda e: e.tensor_copy(beta_b, gb_ps), pV)

        em.wait("scalar", dX, T[("lds",)])
        T[("sgn",)] = em.op("scalar", lambda e: e.activation(sgn_sb, sigs_sb, AF.Sign), pA)
        em.wait("tensor", pA, T[("sgn",)])
        for j in range(NT):
            t_sc = em.op("tensor", lambda e, j=j: e.matmul(
                sc_ps[:, j * 8:(j + 1) * 8], peJ_sb[:, j * 128:(j + 1) * 128],
                sgn_sb, start=(j == 0), stop=(j == NT - 1)), pP)
        em.wait("vector", pP, t_sc)
        t = em.op("vector", lambda e: e.tensor_copy(scores_sb, sc_ps), pV)
        em.wait("vector", pV, t)
        t = em.op("vector", lambda e: e.reduce_max(m_sb, scores_sb, axis=AX.X), pV)
        for t8 in range(8):
            em.wait("vector", pV, t)
            t = em.op("vector", lambda e, t8=t8: e.tensor_tensor(
                eq_sb, scores_sb[:, :, t8], m_sb, ALU.is_equal), pV)
            em.wait("vector", pV, t)
            t = em.op("vector", lambda e, t8=t8: e.tensor_tensor(
                f_sb[:, :, t8], eq_sb, noty_sb, ALU.mult), pV)
            em.wait("vector", pV, t)
            t = em.op("vector", lambda e, t8=t8: e.tensor_tensor(
                noty_sb, noty_sb, f_sb[:, :, t8], ALU.subtract), pV)
        T[("f",)] = t
        em.wait("vector", pV, t)
        t = em.op("vector", lambda e: e.reduce_sum(
            c8_sb, f_sb.rearrange("p a b -> p b a"), axis=AX.X), pV)
        em.wait("vector", pV, t)
        t_cbf = em.op("vector", lambda e: e.tensor_copy(cbf_sb, c8_sb), pV)
        em.wait("tensor", pV, t_cbf)
        t_pt = em.op("tensor", lambda e: e.matmul(pt_ps, cbf_sb, ltri_s,
                                                  start=True, stop=True), pP)
        em.wait("vector", pP, t_pt)
        t_ptev = em.op("vector", lambda e: e.tensor_copy(pt_sb, pt_ps), pV)
        em.wait("vector", pV, t_ptev)
        t = em.op("vector", lambda e: e.tensor_copy(ntc, pt_sb[:, 128:129]), pV)
        em.wait("vector", pV, t)
        t = em.op("vector", lambda e: e.tensor_scalar(thrcnt, thr_s, ntc, None, ALU.is_lt), pV)
        em.wait("vector", pV, t)
        t = em.op("vector", lambda e: e.reduce_sum(padc, thrcnt, axis=AX.X), pV)
        em.wait("vector", pV, t)
        t_ceil = em.op("vector", lambda e: e.tensor_scalar(padc, padc, 128.0, None, ALU.mult), pV)
        em.wait("tensor", pV, t_ceil)
        t_pad = em.op("tensor", lambda e: e.matmul(pad_ps, padc, ltri8_s,
                                                   start=True, stop=True), pP)
        em.wait("tensor", pV, t_ptev)
        t_tr1 = em.op("tensor", lambda e: e.transpose(tr_ps[:, 0:8], pt_sb[:, 0:128],
                                                      idf[0:8, 0:8]), pP)
        em.wait("vector", pP, t_pad)
        t_prow = em.op("vector", lambda e: e.tensor_copy(padrow, pad_ps), pV)
        em.wait("vector", pP, t_tr1)
        t_ptm = em.op("vector", lambda e: e.tensor_copy(ptm, tr_ps[:, 0:8]), pV)
        em.wait("vector", pV, t_prow)
        t_incl = em.op("vector", lambda e: e.tensor_copy(inclrow, padrow[:, 1:9]), pV)
        em.wait("tensor", pV, t_prow)
        t_pb = em.op("tensor", lambda e: e.matmul(padb_ps, ones1f, padrow[:, 0:8],
                                                  start=True, stop=True), pP)
        em.wait("vector", pP, t_pb)
        t = em.op("vector", lambda e: e.tensor_copy(padb, padb_ps), pV)
        em.wait("vector", pV, t)
        em.wait("vector", pV, t_ptm)
        t_pp = em.op("vector", lambda e: e.tensor_tensor(pp_sb, ptm, padb, ALU.add), pV)
        em.wait("vector", pV, T[("f",)])
        for t8 in range(8):
            t = em.op("vector", lambda e, t8=t8: e.tensor_tensor_scan(
                w_sb[:, :, t8], f_sb[:, :, t8], f_sb[:, :, t8], 0.0, ALU.add, ALU.bypass), pV)
        em.wait("vector", pV, t)
        t = em.op("vector", lambda e: e.tensor_tensor(w_sb[:], w_sb[:], f_sb[:], ALU.subtract), pV)
        em.wait("vector", pV, t)
        em.wait("vector", pV, t_pp)
        ppb = pp_sb.rearrange("p t -> p () t").broadcast_to([128, NT, 8])
        t = em.op("vector", lambda e, ppb=ppb: e.tensor_tensor(w3_sb[:], w_sb[:], ppb, ALU.add), pV)
        em.wait("vector", pV, t)
        t = em.op("vector", lambda e: e.tensor_tensor(w3_sb[:], w3_sb[:], f_sb[:], ALU.mult), pV)
        em.wait("vector", pV, t)
        t = em.op("vector", lambda e: e.reduce_sum(dest_sb, w3_sb[:], axis=AX.X), pV)
        em.wait("vector", pV, t)
        T[("destu",)] = em.op("vector", lambda e: e.tensor_copy(destu, dest_sb), pV)

        em.wait("vector", pV, t_incl)
        t = T[("ms",)]
        for t8 in range(7):
            em.wait("vector", pV, t)
            t = em.op("vector", lambda e, t8=t8: e.tensor_scalar(
                tmpr, i128, inclrow[:, t8:t8 + 1], None, ALU.is_ge), pV)
            em.wait("vector", pV, t)
            t = em.op("vector", lambda e: e.tensor_tensor(tid_sb, tid_sb, tmpr, ALU.add), pV)
        em.wait("vector", pV, t)
        T[("tidu",)] = em.op("vector", lambda e: e.tensor_copy(tidu_sb, tid_sb), pV)
        em.wait("vector", pV, t)
        t_t1k = em.op("vector", lambda e: e.tensor_scalar(tmpr, tid_sb, 1024.0, None, ALU.mult), pV)
        em.wait("tensor", pV, t_t1k)
        t_wb = em.op("tensor", lambda e: e.matmul(wix_ps, ones1f, tmpr,
                                                  start=True, stop=True), pP)
        em.wait("vector", pP, t_wb)
        em.wait("vector", dX, T[("lds",)])
        t = em.op("vector", lambda e: e.tensor_scalar(widx_sb, wix_ps, iota8_sb, None, ALU.add), pV)
        em.wait("vector", pV, t)
        T[("widx",)] = em.op("vector", lambda e: e.tensor_copy(widxu_sb, widx_sb), pV)
        em.wait("tensor", pV, T[("widx",)])
        em.wait("tensor", pV, T[("tidu",)])
        t_tb = em.op("tensor", lambda e: e.matmul(wix_ps, ones1f, tid_sb,
                                                  start=True, stop=True), pP)
        em.wait("vector", pP, t_tb)
        T[("tidb",)] = em.op("vector", lambda e: e.tensor_copy(tidbu_sb, wix_ps), pV)

        em.wait("gpsimd", pV, T[("ms",)])
        t_f2 = em.op("gpsimd", lambda e: e.dma_start(out=g_dram, in_=sent_sb), dG)
        em.wait("gpsimd", dG, t_f2)
        em.wait("gpsimd", pV, T[("destu",)])
        gflat = g_dram.rearrange("a b -> (a b) ()")
        mflat = m_dram.rearrange("a b -> (a b) ()")
        for j in range(NT):
            t_sc2 = em.op("gpsimd", lambda e, j=j: e.indirect_dma_start(
                out=gflat, out_offset=IndirectOffsetOnAxis(destu[:, j:j + 1], 0),
                in_=iotaJ[:, j:j + 1], in_offset=None), dG)
        em.wait("gpsimd", dG, t_sc2)
        t_mld = em.op("gpsimd", lambda e: e.dma_start(out=gld_sb, in_=g_dram), dG)
        em.wait("tensor", dG, t_mld)
        em.wait("tensor", pV, t_ptm)
        t_tr2 = em.op("tensor", lambda e: e.transpose(tr_ps[:, 0:64], gld_sb,
                                                      idf[0:64, 0:64]), pP)
        em.wait("vector", pP, t_tr2)
        t_gsb = em.op("vector", lambda e: e.tensor_copy(g_sb, tr_ps[:, 0:64]), pV)
        em.wait("vector", pV, t_gsb)
        T[("g",)] = em.op("vector", lambda e: e.tensor_copy(gu_sb, g_sb), pV)
        t_gs = em.op("vector", lambda e: e.tensor_scalar(
            gsafe_sb, g_sb, float(NTOK - 1), None, ALU.min), pV)
        em.wait("vector", pV, t_gs)
        T[("gsafe",)] = em.op("vector", lambda e: e.tensor_copy(gusafe_sb, gsafe_sb), pV)
        em.wait("vector", pV, t_gsb)
        T[("msk",)] = em.op("vector", lambda e: e.tensor_scalar(
            msk_sb, g_sb, float(SENT), None, ALU.is_lt), pV)

        if "f" in DBG:
            em.wait("sync", pV, T[("f",)])
            em.op("sync", lambda e, d=dbg_out["f"]: e.dma_start(out=d, in_=f_sb[:]), dO)
        if "dest" in DBG:
            em.wait("sync", pV, T[("destu",)])
            em.op("sync", lambda e, d=dbg_out["dest"]: e.dma_start(out=d, in_=dest_sb), dO)
        if "g" in DBG:
            em.wait("sync", pV, T[("g",)])
            em.op("sync", lambda e, d=dbg_out["g"]: e.dma_start(out=d, in_=g_sb), dO)
        if "tid" in DBG:
            em.wait("sync", pV, T[("tidu",)])
            em.op("sync", lambda e, d=dbg_out["tid"]: e.dma_start(out=d, in_=tid_sb), dO)
        if "widx" in DBG:
            em.wait("sync", pV, T[("widx",)])
            em.op("sync", lambda e, d=dbg_out["widx"]: e.dma_start(out=d, in_=widx_sb), dO)
        if "msk" in DBG:
            em.wait("sync", pV, T[("msk",)])
            em.op("sync", lambda e, d=dbg_out["msk"]: e.dma_start(out=d, in_=msk_sb), dO)

        ph0 = dict(pP=pP.n, pA=pA.n, pV=pV.n, dX=dX.n, dG=dG.n, dO=dO.n)
    es0p.close()
    if STOP == 0:
        return finish()
    with nc.sbuf_tensor("wk_sb", [128, NDC * D], BF16) as _wk, \
         nc.sbuf_tensor("wv_sb", [128, NDC * D], BF16) as _wv, \
         nc.sbuf_tensor("wtmp_sb", [128, 2, 2 * D], BF16) as _wtmp, \
         nc.sbuf_tensor("x_sb", [128, 2, D], F32) as _x, \
         nc.sbuf_tensor("xnbf", [128, 3, D], BF16) as _xnbf, \
         nc.sbuf_tensor("sq1_scr", [128, D], BF16) as _sq1, \
         nc.sbuf_tensor("st1", [128, 16], F32) as _st1, \
         nc.sbuf_tensor("k_bf", [128, D], BF16) as _kbf, \
         nc.sbuf_tensor("v_aug", [128, NG, 130], BF16) as _vaug, \
         nc.psum_tensor("kv_ps", [128, 1536], F32) as _kv_ps, \
         nc.psum_tensor("k_ps", [128, D], F32) as _k_ps, \
         nc.psum_tensor("v_ps", [128, D], F32) as _v_ps:
        wk_sb = _wk[:].rearrange("p (dc o) -> p dc o", dc=NDC)
        wv_sb = _wv[:].rearrange("p (dc o) -> p dc o", dc=NDC)
        wtmp, x_sb, xnbf, sq1_scr, st1 = _wtmp[:], _x[:], _xnbf[:], _sq1[:], _st1[:]
        k_bf, v_aug = _kbf[:], _vaug[:]
        kv_ps, k_ps, v_ps = _kv_ps[:], _k_ps[:], _v_ps[:]
        stc = lambda idx, pb: st1[:, 2 * idx + pb:2 * idx + pb + 1]
        kvoff = lambda g: (g // 3) * 512 + (g % 3) * 130

        for ctr, key in ((pP, "pP"), (pA, "pA"), (pV, "pV"), (dX, "dX"), (dO, "dO")):
            em.wait("gpsimd", ctr, ph0[key])
        em.wait("gpsimd", dG, ph0["dG"])
        t_wk = em.op("gpsimd", lambda e: e.dma_start(out=wk_sb, in_=wkT), dWT)
        t_wv = em.op("gpsimd", lambda e: e.dma_start(out=wv_sb, in_=wvT), dWT)
        em.wait("vector", pV, T[("ms",)])
        t_vone = em.op("vector", lambda e: e.memset(v_aug, 1.0), pV)
        t_kvz = em.op("vector", lambda e: e.memset(kvblk, 0.0), pV)
        t_xz = em.op("vector", lambda e: e.memset(x_sb, 0.0), pV)
        em.wait("gpsimd", pV, t_xz)

        def gather_fe(ig):
            if ig >= 2:
                em.wait("gpsimd", pA, T[("sq", ig - 2)])
                em.wait("gpsimd", pV, T[("xn", ig - 2)])
            em.wait("gpsimd", pV, T[("gsafe",)])
            if ig >= 1:
                em.wait("gpsimd", dGX, T[("xg", ig - 1)])
            T[("xg", ig)] = em.op("gpsimd", lambda e, ig=ig, pg=ig % 2: e.indirect_dma_start(
                out=x_sb[:, pg, :], out_offset=None,
                in_=x_in, in_offset=IndirectOffsetOnAxis(gusafe_sb[:, ig:ig + 1], 0)), dGX)

        def norm_fe(ig):
            pg = ig % 2
            em.wait("scalar", dGX, T[("xg", ig)])
            T[("sq", ig)] = em.op("scalar", lambda e, pg=pg: e.activation(
                sq1_scr, x_sb[:, pg, :], AF.Square, accum_out=stc(0, pg)), pA)
            em.wait("vector", dGX, T[("xg", ig)])
            t = em.op("vector", lambda e, pg=pg: e.reduce_sum(stc(1, pg), x_sb[:, pg, :], axis=AX.X), pV)
            em.wait("vector", pV, t)
            t = em.op("vector", lambda e, pg=pg: e.tensor_scalar(stc(2, pg), stc(1, pg), float(RD), None, ALU.mult), pV)
            em.wait("vector", pV, t)
            t = em.op("vector", lambda e, pg=pg: e.tensor_tensor(stc(3, pg), stc(2, pg), stc(2, pg), ALU.mult), pV)
            em.wait("vector", pA, T[("sq", ig)])
            t2 = em.op("vector", lambda e, pg=pg: e.tensor_scalar(stc(4, pg), stc(0, pg), float(RD), None, ALU.mult), pV)
            em.wait("vector", pV, t2)
            t3 = em.op("vector", lambda e, pg=pg: e.tensor_tensor(stc(5, pg), stc(4, pg), stc(3, pg), ALU.subtract), pV)
            em.wait("scalar", pV, t3)
            T[("sd", ig)] = em.op("scalar", lambda e, pg=pg: e.activation(
                stc(6, pg), stc(5, pg), AF.Sqrt, bias=eps_sb), pA)
            em.wait("vector", pA, T[("sd", ig)])
            t4 = em.op("vector", lambda e, pg=pg: e.reciprocal(stc(7, pg), stc(6, pg)), pV)
            em.wait("vector", pV, t4)
            if ig >= 3:
                em.wait("vector", dT, T[("trs", ig - 3)])
                em.wait("vector", dO, T[("spill", ig - 3)])
            T[("xn", ig)] = em.op("vector", lambda e, pg=pg, p3=ig % 3: e.tensor_scalar(
                xnbf[:, p3, :], x_sb[:, pg, :], stc(2, pg), stc(7, pg),
                ALU.subtract, ALU.mult), pV)

            em.wait("scalar", pV, T[("xn", ig)])
            if ig >= 1:
                em.wait("scalar", dO, T[("spill", ig - 1)])
            T[("spill", ig)] = em.op("scalar", lambda e, ig=ig, p3=ig % 3: e.dma_start(
                out=xn_dram[ig * 128:(ig + 1) * 128, :], in_=xnbf[:, p3, :]), dO)
            em.wait("sync", pV, T[("xn", ig)])
            for dc in range(NDC):
                trs = em.op("sync", lambda e, dc=dc, p3=ig % 3, ig=ig: e.dma_start(
                    out=xnT[dc][:, ig * 128:(ig + 1) * 128],
                    in_=xnbf[:, p3, dc * 128:(dc + 1) * 128], transpose=True), dT)
                T[("trs", ig, dc)] = trs
            T[("trs", ig)] = trs

        for ig in range(min(2, NRT)):
            gather_fe(ig)
        if NRT >= 1:
            norm_fe(0)
        for i in range(NRT):
            pb = i % 2
            if i + 2 < NRT:
                gather_fe(i + 2)
            if i + 1 < NRT:
                norm_fe(i + 1)
            cpt = -(-32 // NRT)
            for ck in range(i * cpt, min((i + 1) * cpt, 32)):
                t8, ci = ck // 4, ck % 4
                wb_ = ck % 2
                if ck >= 1:
                    em.wait("gpsimd", dG, T[("wfst", ck - 1)])
                T[("wfld", ck)] = em.op("gpsimd", lambda e, t8=t8, ci=ci, wb_=wb_: e.dma_start(
                    out=wtmp[:, wb_, :].rearrange("p (a o) -> p a o", a=2),
                    in_=wfT[t8, :, 2 * ci:2 * ci + 2, :]), dG)
                em.wait("gpsimd", dG, T[("wfld", ck)])
                T[("wfst", ck)] = em.op("gpsimd", lambda e, t8=t8, ci=ci, wb_=wb_: e.dma_start(
                    out=wf_bf[t8, :, 2 * ci:2 * ci + 2, :],
                    in_=wtmp[:, wb_, :].rearrange("p (a o) -> p a o", a=2)), dG)

            em.wait("tensor", pV, ph0["pV"])
            em.wait("tensor", dWT, t_wk)
            em.wait("tensor", dG, T[("bkr",)])
            em.wait("tensor", pV, T[("ms",)])
            if i >= 1:
                em.wait("tensor", pA, T[("krel", i - 1)])
            for dc in range(NDC):
                em.wait("tensor", dT, T[("trs", i, dc)])
                for hf in range(2):
                    sl = slice(hf * 512, hf * 512 + 512)
                    em.op("tensor", lambda e, dc=dc, sl=sl, i=i: e.matmul(
                        k_ps[:, sl], xnT[dc][:, i * 128:(i + 1) * 128],
                        wk_sb[:, dc, sl], start=(dc == 0), stop=False))
            for hf in range(2):
                sl = slice(hf * 512, hf * 512 + 512)
                tkmm = em.op("tensor", lambda e, sl=sl: e.matmul(
                    k_ps[:, sl], ones1b, bkr_sb[:, sl], start=False, stop=True), pP)
            T[("kmm", i)] = tkmm
            em.wait("scalar", pP, T[("kmm", i)])
            em.wait("scalar", pV, T[("msk",)])
            T[("krel", i)] = em.op("scalar", lambda e, i=i: e.activation(
                k_bf, k_ps, AF.Relu, scale=msk_sb[:, i:i + 1]), pA)
            em.wait("tensor", dWT, t_wv)
            em.wait("tensor", dG, T[("bvr",)])
            if i >= 1:
                em.wait("tensor", pV, T[("vaug", i - 1)])
            for dc in range(NDC):
                for hf in range(2):
                    sl = slice(hf * 512, hf * 512 + 512)
                    em.op("tensor", lambda e, dc=dc, sl=sl, i=i: e.matmul(
                        v_ps[:, sl], xnT[dc][:, i * 128:(i + 1) * 128],
                        wv_sb[:, dc, sl], start=(dc == 0), stop=False))
            for hf in range(2):
                sl = slice(hf * 512, hf * 512 + 512)
                tvmm = em.op("tensor", lambda e, sl=sl: e.matmul(
                    v_ps[:, sl], ones1b, bvr_sb[:, sl], start=False, stop=True), pP)
            T[("vmm", i)] = tvmm

            em.wait("vector", pP, T[("vmm", i)])
            em.wait("vector", pV, t_vone)
            em.wait("vector", pV, T[("msk",)])
            v3 = v_ps.rearrange("p (g e) -> p g e", g=NG)
            t_va = em.op("vector", lambda e, i=i, v3=v3: e.tensor_scalar(
                v_aug[:, :, 0:64], v3[:, :, 0:64], msk_sb[:, i:i + 1], None, ALU.mult), pV)
            em.wait("vector", pV, t_va)
            T[("vaug", i)] = em.op("vector", lambda e, i=i, v3=v3: e.tensor_scalar(
                v_aug[:, :, 65:129], v3[:, :, 64:128], msk_sb[:, i:i + 1], None, ALU.mult), pV)

            if i == 0 and "k0" in DBG:
                em.wait("sync", pA, T[("krel", 0)])
                em.op("sync", lambda e, d=dbg_out["k0"]: e.dma_start(out=d, in_=k_bf), dO)
            if i == 0 and "xn0" in DBG:
                em.wait("sync", pV, T[("xn", 0)])
                em.op("sync", lambda e, d=dbg_out["xn0"]: e.dma_start(out=d, in_=xnbf[:, 0, :]), dO)
            em.wait("tensor", pA, T[("krel", i)])
            em.wait("tensor", pV, T[("vaug", i)])
            for g in range(NG):
                st = (i == 0 and g % 3 == 0)
                sp = (i == NRT - 1 and (g % 3 == 2 or g == NG - 1))
                tkv = em.op("tensor", lambda e, g=g, st=st, sp=sp: e.matmul(
                    kv_ps[:, kvoff(g):kvoff(g) + 130],
                    k_bf[:, g * 128:(g + 1) * 128], v_aug[:, g, :],
                    start=st, stop=sp), pP)
            T[("kv", i)] = tkv

        if "xnt0" in DBG:
            em.wait("sync", pV, T[("tev", 0)])
            em.op("sync", lambda e, d=dbg_out["xnt0"]: e.dma_start(
                out=d, in_=xnT[0][:, 0:256]), dO)
        em.wait("vector", pP, T[("kv", NRT - 1)])
        em.wait("vector", pV, t_kvz)
        for g in range(NG):
            off = kvoff(g)
            em.op("vector", lambda e, g=g, off=off: e.tensor_copy(
                kvblk[0:64, g, 0:65], kv_ps[0:64, off:off + 65]))
            t_kvb = em.op("vector", lambda e, g=g, off=off: e.tensor_copy(
                kvblk[64:128, g, 65:130], kv_ps[64:128, off + 65:off + 130]), pV)
        T[("kvblk",)] = t_kvb
        if "kv" in DBG:
            t_kd = em.op("vector", lambda e: e.tensor_copy(kvdbg, kv_ps), pV)
            em.wait("sync", pV, t_kd)
            em.op("sync", lambda e, d=dbg_out["kv"]: e.dma_start(out=d, in_=kvdbg), dO)
        ph1 = dict(pP=pP.n, pA=pA.n, pV=pV.n, dT=dT.n, dX=dX.n, dG=dG.n, dO=dO.n)
    if STOP == 1:
        return finish()
    segs = []
    off = 0
    while off < NTS:
        n = min(2048, NTS - off)
        segs.append((off, n))
        off += n
    with nc.sbuf_tensor("wq_sb", [128, NDC * D], BF16) as _wq, \
         nc.sbuf_tensor("qseg", [128, 2, 2048], BF16) as _qseg, \
         nc.psum_tensor("q_ps", [128, 2, 2048], F32) as _q_ps:
        wq_sb = _wq[:].rearrange("p (dc o) -> p dc o", dc=NDC)
        qseg, q_ps = _qseg[:], _q_ps[:]
        for ctr, v in ((pP, ph1["pP"]), (pA, ph1["pA"]), (pV, ph1["pV"]),
                       (dT, ph1["dT"]), (dO, ph1["dO"]), (dX, ph1["dX"]),
                       (dGX, dGX.n), (dG, dG.n)):
            em.wait("gpsimd", ctr, v)
        t_wq = em.op("gpsimd", lambda e: e.dma_start(out=wq_sb, in_=wqT), dWT)
        em.wait("tensor", pV, T[("kvblk",)])
        em.wait("tensor", pA, T[("krel", NRT - 1)])
        em.wait("tensor", dWT, t_wq)
        nq = 0
        for si, (soff, sn) in enumerate(segs):
            for oc in range(8):
                sp = nq % 2
                if nq >= 2:
                    em.wait("tensor", pA, T[("qev", nq - 2)])
                for dc in range(NDC):
                    co = 0
                    while co < sn:
                        cn = min(512, sn - co)
                        tq = em.op("tensor", lambda e, oc=oc, dc=dc, sp=sp, soff=soff, co=co, cn=cn: e.matmul(
                            q_ps[:, sp, co:co + cn],
                            wq_sb[:, dc, oc * 128:(oc + 1) * 128],
                            xnT[dc][:, soff + co:soff + co + cn],
                            start=(dc == 0), stop=(dc == NDC - 1)), pP)
                        co += cn
                T[("qmm", nq)] = tq
                em.wait("scalar", pP, tq)
                if nq >= 2:
                    em.wait("scalar", dO, T[("qst", nq - 2)])
                T[("qev", nq)] = em.op("scalar", lambda e, oc=oc, sp=sp, sn=sn: e.activation(
                    qseg[:, sp, 0:sn], q_ps[:, sp, 0:sn], AF.Relu,
                    bias=bqc_sb[:, oc:oc + 1]), pA)
                em.wait("sync", pA, T[("qev", nq)])
                if nq >= 1:
                    em.wait("sync", dO, T[("qst", nq - 1)])
                T[("qst", nq)] = em.op("sync", lambda e, oc=oc, sp=sp, soff=soff, sn=sn: e.dma_start(
                    out=q_dram[oc][:, soff:soff + sn], in_=qseg[:, sp, 0:sn]), dO)
                nq += 1
        ph3 = dict(pP=pP.n, pA=pA.n, pV=pV.n, dO=dO.n, dG=dG.n)
    if STOP == 3:
        return finish()

    NBLK = 4
    with nc.sbuf_tensor("wsel", [128, 3, NDC * D], BF16) as _wsel, \
         nc.sbuf_tensor("bsel", [128, 3, D], BF16) as _bsel, \
         nc.sbuf_tensor("qblk", [128, 2, NG, 512], BF16) as _qblk, \
         nc.sbuf_tensor("x2_sb", [128, 3, D], F32) as _x2, \
         nc.sbuf_tensor("xn2", [128, 2, D], BF16) as _xn2, \
         nc.sbuf_tensor("y_sb", [128, D], F32) as _y, \
         nc.sbuf_tensor("o_sb", [128, 2, D], F32) as _o, \
         nc.sbuf_tensor("sq2_scr", [128, D], BF16) as _sq2, \
         nc.sbuf_tensor("dmx_sb", [128, 16], F32) as _dmx, \
         nc.sbuf_tensor("dinv_sb", [128, 16], F32) as _dinv, \
         nc.sbuf_tensor("st2", [128, 16], F32) as _st2, \
         nc.psum_tensor("ctx_ps", [128, 1536], F32) as _ctx_ps, \
         nc.psum_tensor("ffn_ps", [128, 2, D], F32) as _ffn_ps:
        wsel = _wsel[:].rearrange("p b (dc o) -> p b dc o", dc=NDC)
        wsel2d = _wsel[:]
        bsel, qblk = _bsel[:], _qblk[:]
        x2_sb, xn2, y_sb, o_sb, sq2_scr = _x2[:], _xn2[:], _y[:], _o[:], _sq2[:]
        dmx_sb, dinv_sb, st2 = _dmx[:], _dinv[:], _st2[:]
        ctx_ps, ffn_ps = _ctx_ps[:], _ffn_ps[:]
        coff = lambda g: (g // 3) * 512 + (g % 3) * 130
        s2 = lambda idx: st2[:, idx:idx + 1]
        wf_flat = wf_bf_flat

        ph3f = dict(pP=pP.n, pA=pA.n, pV=pV.n, dX=dX.n, dG=dG.n, dT=dT.n, dO=dO.n,
                    dGX=dGX.n, dWT=dWT.n)
        for eng in ("sync", "gpsimd"):
            for ctr, key in ((pP, "pP"), (pA, "pA"), (pV, "pV"), (dX, "dX"),
                             (dT, "dT"), (dO, "dO"), (dG, "dG"), (dGX, "dGX"),
                             (dWT, "dWT")):
                em.wait(eng, ctr, ph3f[key])
        em.wait("tensor", pA, ph3f["pA"])
        em.wait("tensor", pV, ph3f["pV"])

        def emit_gathers(ig):
            em.wait("gpsimd", pV, T[("widx",)])
            em.wait("gpsimd", pV, T[("tidu",)])
            em.wait("gpsimd", dG, T[("bias_bf",)])
            em.wait("gpsimd", dG, T[("wfst", 31)])
            if ig >= 3 and "ffn" not in SKIP:
                em.wait("gpsimd", pP, T[("fmm", ig - 3)])
            if "wg" not in SKIP:
                dgw = dGW0 if ig % 2 == 0 else dGW1
                if ig >= 2:
                    em.wait("gpsimd", dgw, T[("wg", ig - 2)][1])
                T[("wg", ig)] = (dgw, em.op("gpsimd", lambda e, ig=ig, wb=ig % 3: e.indirect_dma_start(
                    out=wsel2d[:, wb, :], out_offset=None,
                    in_=wf_flat, in_offset=IndirectOffsetOnAxis(widxu_sb[:, ig:ig + 1], 0)), dgw))
            em.wait("gpsimd", pV, T[("tidb",)])
            if ig >= 1:
                em.wait("gpsimd", dB, T[("bg", ig - 1)])
            T[("bg", ig)] = em.op("gpsimd", lambda e, ig=ig, wb=ig % 3: e.indirect_dma_start(
                out=bsel[:, wb, :], out_offset=None,
                in_=bias_bf, in_offset=IndirectOffsetOnAxis(tidbu_sb[:, ig:ig + 1], 0)), dB)
            if ig >= 3:
                em.wait("gpsimd", pV, T[("fin", ig - 3)])
            em.wait("gpsimd", pV, T[("gsafe",)])
            dx2 = dX2a if ig % 2 == 0 else dX2b
            if ig >= 2:
                em.wait("gpsimd", dx2, T[("x2", ig - 2)][1])
            T[("x2", ig)] = (dx2, em.op("gpsimd", lambda e, ig=ig, p3=ig % 3: e.indirect_dma_start(
                out=x2_sb[:, p3, :], out_offset=None,
                in_=x_in, in_offset=IndirectOffsetOnAxis(gusafe_sb[:, ig:ig + 1], 0)), dx2))

        def emit_qbl(b):
            qb0 = b * NBLK * 128
            qbn = min(NBLK * 128, NTS - qb0)
            if b >= 2 and "ctx" not in SKIP:
                em.wait("sync", pP, T[("ctx", (b - 1) * NBLK - 1)])
            if b >= 1:
                em.wait("sync", dQ, T[("qbl", b - 1)])
            T[("qbl", b)] = em.op("sync", lambda e, qb0=qb0, qbn=qbn, b2=b % 2: e.dma_start(
                out=qblk[:, b2, :, 0:qbn],
                in_=q_dram.rearrange("oc p n -> p oc n")[:, :, qb0:qb0 + qbn]), dQ)

        for ig in range(min(3, NRT)):
            emit_gathers(ig)
        emit_qbl(0)
        for i in range(NRT):
            pb = i % 2
            blk = i // NBLK
            bb = blk % 2
            if i >= 2:
                em.wait("sync", pV, T[("yxn", i - 2)])
            dxn = dXNa if i % 2 == 0 else dXNb
            if i >= 2:
                em.wait("sync", dxn, T[("xnl", i - 2)][1])
            T[("xnl", i)] = (dxn, em.op("sync", lambda e, i=i, pb=pb: e.dma_start(
                out=xn2[:, pb, :], in_=xn_dram[i * 128:(i + 1) * 128, :]), dxn))
            if i % NBLK == 0 and (blk + 1) * NBLK * 128 < NTS + NBLK * 128:
                nb = blk + 1
                if nb * NBLK * 128 < NTS:
                    emit_qbl(nb)

            if i == 0 and "xn20" in DBG:
                em.wait("sync", dX, T[("xnl", 0)])
                em.op("sync", lambda e, d=dbg_out["xn20"]: e.dma_start(out=d, in_=xn2[:, 0, :]), dO)
            if i == 0 and "qb0" in DBG:
                em.wait("sync", dQ, T[("qbl", 0)])
                em.op("sync", lambda e, d=dbg_out["qb0"]: e.dma_start(
                    out=d, in_=qblk[:, 0, 0:1, :].rearrange("p a n -> p (a n)")), dO)
            em.wait("tensor", dQ, T[("qbl", blk)])
            em.wait("tensor", pV, T[("kvblk",)])
            if i >= 1 and ("ctx" not in SKIP):
                em.wait("tensor", pV, T[("yc", i - 1)])
            for g in (range(NG) if "ctx" not in SKIP else []):
                stf = (g % 3 == 0)
                spf = (g % 3 == 2 or g == NG - 1)
                tc = em.op("tensor", lambda e, g=g, i=i, bb=bb, stf=stf, spf=spf: e.matmul(
                    ctx_ps[:, coff(g):coff(g) + 130],
                    qblk[:, bb, g, (i % NBLK) * 128:(i % NBLK) * 128 + 128],
                    kvblk[:, g, :], start=stf, stop=spf), pP)
            if "ctx" not in SKIP:
                T[("ctx", i)] = tc
            if "ffn" not in SKIP:
                em.wait("tensor", T[("wg", i)][0], T[("wg", i)][1])
                em.wait("tensor", dB, T[("bg", i)])
                if i >= 2:
                    em.wait("tensor", pV, T[("ffnadd", i - 2)])
                em.wait("tensor", dG, T[("berbf",)])
                for dc in range(NDC):
                    for hf in range(2):
                        sl = slice(hf * 512, hf * 512 + 512)
                        em.op("tensor", lambda e, dc=dc, sl=sl, i=i, pb=pb, wb=i % 3: e.matmul(
                            ffn_ps[:, pb, sl], xnT[dc][:, i * 128:(i + 1) * 128],
                            wsel[:, wb, dc, sl], start=(dc == 0), stop=False))
                em.wait("tensor", dG, T[("berbf",)])
                for hf in range(2):
                    sl = slice(hf * 512, hf * 512 + 512)
                    em.op("tensor", lambda e, sl=sl, pb=pb, wb=i % 3: e.matmul(
                        ffn_ps[:, pb, sl], ones1b, bsel[0:1, wb, sl],
                        start=False, stop=False))
                for hf in range(2):
                    sl = slice(hf * 512, hf * 512 + 512)
                    tf = em.op("tensor", lambda e, sl=sl, pb=pb: e.matmul(
                        ffn_ps[:, pb, sl], ones1b, ber_bf[:, sl],
                        start=False, stop=True), pP)
                T[("fmm", i)] = tf

            if "ctx" in SKIP:
                T[("yc", i)] = em.op("vector", lambda e: e.memset(y_sb, 0.0), pV)
            else:
                em.wait("vector", pP, T[("ctx", i)])
            if i >= 1:
                em.wait("vector", pV, T[("fin", i - 1)])
            for b3 in (range(3) if "ctx" not in SKIP else []):
                nblk = 6 if b3 < 2 else 4
                dsrc = ctx_ps[:, b3 * 512:b3 * 512 + 65 * nblk].rearrange(
                    "p (n e) -> p n e", e=65)[:, :, 64:65]
                t5 = em.op("vector", lambda e, dsrc=dsrc, b3=b3, nblk=nblk: e.tensor_scalar(
                    dmx_sb[:, 6 * b3:6 * b3 + nblk], dsrc, EPS_DEN, None, ALU.max), pV)
            if "ctx" not in SKIP:
                em.wait("vector", pV, t5)
                t6 = em.op("vector", lambda e: e.reciprocal(dinv_sb, dmx_sb), pV)
                em.wait("vector", pV, t6)
            for b3 in (range(3) if "ctx" not in SKIP else []):
                nblk = 6 if b3 < 2 else 4
                csrc = ctx_ps[:, b3 * 512:b3 * 512 + 65 * nblk].rearrange(
                    "p (n e) -> p n e", e=65)[:, :, 0:64]
                dsl = dinv_sb[:, 6 * b3:6 * b3 + nblk].rearrange(
                    "p n -> p n ()").broadcast_to([128, nblk, 64])
                t7 = em.op("vector", lambda e, csrc=csrc, dsl=dsl, b3=b3, nblk=nblk: e.tensor_tensor(
                    y_sb[:, 384 * b3:384 * b3 + 64 * nblk].rearrange("p (n e) -> p n e", e=64),
                    csrc, dsl, ALU.mult), pV)
            if "ctx" not in SKIP:
                T[("yc", i)] = t7
            em.wait("vector", pV, T[("yc", i)])
            em.wait("vector", T[("xnl", i)][0], T[("xnl", i)][1])
            t8_ = em.op("vector", lambda e, pb=pb: e.tensor_tensor(
                y_sb, y_sb, xn2[:, pb, :], ALU.add), pV)
            T[("yxn", i)] = t8_
            em.wait("scalar", pV, t8_)
            T[("sq2", i)] = em.op("scalar", lambda e: e.activation(
                sq2_scr, y_sb, AF.Square, accum_out=s2(0)), pA)
            em.wait("vector", pV, t8_)
            t = em.op("vector", lambda e: e.reduce_sum(s2(1), y_sb, axis=AX.X), pV)
            em.wait("vector", pV, t)
            t = em.op("vector", lambda e: e.tensor_scalar(s2(2), s2(1), float(RD), None, ALU.mult), pV)
            em.wait("vector", pV, t)
            t = em.op("vector", lambda e: e.tensor_tensor(s2(3), s2(2), s2(2), ALU.mult), pV)
            em.wait("vector", pA, T[("sq2", i)])
            t2 = em.op("vector", lambda e: e.tensor_scalar(s2(4), s2(0), float(RD), None, ALU.mult), pV)
            em.wait("vector", pV, t2)
            t3 = em.op("vector", lambda e: e.tensor_tensor(s2(5), s2(4), s2(3), ALU.subtract), pV)
            em.wait("scalar", pV, t3)
            T[("sd2", i)] = em.op("scalar", lambda e: e.activation(
                s2(6), s2(5), AF.Sqrt, bias=eps_sb), pA)
            em.wait("vector", pA, T[("sd2", i)])
            t4 = em.op("vector", lambda e: e.reciprocal(s2(7), s2(6)), pV)
            em.wait("vector", pV, t4)
            t9 = em.op("vector", lambda e: e.tensor_scalar(
                y_sb, y_sb, s2(2), s2(7), ALU.subtract, ALU.mult), pV)
            em.wait("vector", pV, t9)
            em.wait("vector", pV, T[("gb",)])
            t11 = em.op("vector", lambda e: e.tensor_tensor(y_sb, y_sb, gamma_b, ALU.mult), pV)
            em.wait("vector", pV, t11)
            if "ffn" not in SKIP:
                em.wait("vector", pP, T[("fmm", i)])
                t12 = em.op("vector", lambda e, pb=pb: e.tensor_tensor(
                    y_sb, y_sb, ffn_ps[:, pb, :], ALU.add), pV)
                T[("ffnadd", i)] = t12
                em.wait("vector", pV, t12)
            em.wait("vector", T[("x2", i)][0], T[("x2", i)][1])
            if i >= 2:
                em.wait("vector", dG, T[("ost", i - 2)])
            T[("fin", i)] = em.op("vector", lambda e, pb=pb, p3=i % 3: e.tensor_tensor(
                o_sb[:, pb, :], y_sb, x2_sb[:, p3, :], ALU.add), pV)
            em.wait("gpsimd", pV, T[("fin", i)])
            T[("ost", i)] = em.op("gpsimd", lambda e, i=i, pb=pb: e.indirect_dma_start(
                out=y_out, out_offset=IndirectOffsetOnAxis(gu_sb[:, i:i + 1], 0),
                in_=o_sb[:, pb, :], in_offset=None,
                bounds_check=NTOK - 1, oob_is_err=False), dG)
            if i + 3 < NRT:
                emit_gathers(i + 3)

        em.wait("gpsimd", dG, dG.n)
        em.wait("gpsimd", dGW0, dGW0.n)
        em.wait("gpsimd", dGW1, dGW1.n)
        em.wait("gpsimd", dX2a, dX2a.n)
        em.wait("gpsimd", dX2b, dX2b.n)
        em.wait("gpsimd", dB, dB.n)
        em.wait("gpsimd", dGX, dGX.n)
        em.wait("gpsimd", dWT, dWT.n)
        em.wait("sync", dXNa, dXNa.n)
        em.wait("sync", dXNb, dXNb.n)
        em.wait("scalar", dO, dO.n)
        em.wait("sync", dQ, dQ.n)

        with nc.Block() as block:
            for ename in ENGINES:
                getattr(block, ename)(
                    lambda eng, el=tuple(em.steps[ename]): [fn(eng) for fn in el])

    es.close()
    return nc


def host_prep(Wq, bq, Wk, bk, Wv, bv, gamma, beta, sigs, Wf, bf, NT=32):
    NTOK = NT * 128
    NRT = NT + 7
    def chunks(WT):
        return np.ascontiguousarray(WT.reshape(NDC, 128, D).transpose(1, 0, 2))
    pe = make_pe_np()[:NTOK]
    d = {
        "wkT": chunks(np.ascontiguousarray(Wk.T)),
        "wvT": chunks(np.ascontiguousarray(Wv.T)),
        "wqT": chunks(np.ascontiguousarray(Wq.T)),
        "wfT": np.ascontiguousarray(np.stack(
            [chunks(np.ascontiguousarray(Wf[t].T)) for t in range(8)])),
        "bk_row": np.ascontiguousarray(bk.reshape(1, D)),
        "bv_row": np.ascontiguousarray(bv.reshape(1, D)),
        "bq_col": np.ascontiguousarray(bq.reshape(8, 128).T),
        "g_row": np.ascontiguousarray(gamma.reshape(1, D)),
        "be_row": np.ascontiguousarray(beta.reshape(1, D)),
        "bffn": np.ascontiguousarray(bf),
        "sigsT": np.ascontiguousarray(sigs.T),
        "peJ": np.ascontiguousarray(pe.reshape(128, NT, 32).transpose(2, 1, 0)),
        "ltri": np.asarray(np.concatenate(
            [np.triu(np.ones((128, 128), np.float32), 1),
             np.ones((128, 1), np.float32)], 1), dtype=ml_dtypes.bfloat16),
        "ltri8": np.ascontiguousarray(np.concatenate(
            [np.triu(np.ones((8, 8), np.float32), 1),
             np.ones((8, 1), np.float32)], 1)),
        "iota_pj": np.ascontiguousarray(
            (np.arange(128)[:, None] * NT + np.arange(NT)[None, :]).astype(np.float32)),
        "i128row": np.ascontiguousarray(
            (np.arange(NRT) * 128.0).astype(np.float32).reshape(1, NRT)),
        "thr832": np.ascontiguousarray(np.broadcast_to(
            (np.arange(32) * 128.0).astype(np.float32), (8, 32)).copy()),
        "iota_p8": np.ascontiguousarray(
            (np.arange(128) * 8.0).astype(np.float32).reshape(128, 1)),
        "identf": np.eye(128, dtype=np.float32),
        "identb": np.eye(128, dtype=ml_dtypes.bfloat16),
    }
    return d


def kernel(L0_attn_x, L0_attn_Wq, L0_attn_bq, L0_attn_Wk, L0_attn_bk,
           L0_attn_Wv, L0_attn_bv, L0_attn_gamma, L0_attn_beta,
           L0_ffn_tile_sigs, L0_ffn_W, L0_ffn_b):
    x = np.ascontiguousarray(np.asarray(L0_attn_x, dtype=np.float32))
    prep = host_prep(np.asarray(L0_attn_Wq, np.float32), np.asarray(L0_attn_bq, np.float32),
                     np.asarray(L0_attn_Wk, np.float32), np.asarray(L0_attn_bk, np.float32),
                     np.asarray(L0_attn_Wv, np.float32), np.asarray(L0_attn_bv, np.float32),
                     np.asarray(L0_attn_gamma, np.float32), np.asarray(L0_attn_beta, np.float32),
                     np.asarray(L0_ffn_tile_sigs, np.float32), np.asarray(L0_ffn_W, np.float32),
                     np.asarray(L0_ffn_b, np.float32))
    nc = build_program(NT=32)
    in_maps = []
    for core in range(8):
        m = dict(prep)
        m["x_in"] = np.ascontiguousarray(x[core])
        in_maps.append(m)
    res = run_bass_kernel_spmd(nc, in_maps, core_ids=list(range(8)))
    return np.stack([r["y_out"] for r in res.results], axis=0)


# version 14
# speedup vs baseline: 1.1353x; 1.1353x over previous
import numpy as np
import ml_dtypes
import concourse.bass as bass
import concourse.mybir as mybir
from concourse.bass_core_rust import IndirectOffsetOnAxis
from concourse.bass_utils import run_bass_kernel_spmd

F32 = mybir.dt.float32
BF16 = mybir.dt.bfloat16
U32 = mybir.dt.uint32
AF = mybir.ActivationFunctionType
ALU = mybir.AluOpType
AX = mybir.AxisListType

B, S, D, H, HD = 8, 4096, 1024, 16, 64
NDC = 8
NG = 8
EPS_LN = 1e-5
EPS_DEN = 1e-6
RD = np.float32(1.0 / D)
SENT = 16777216.0
ENGINES = ("sync", "gpsimd", "tensor", "scalar", "vector")


def make_pe_np(seq_len=S, d_pos=32):
    import math
    pos = np.arange(seq_len, dtype=np.float32)[:, None]
    div = np.exp(np.arange(0, d_pos, 2, dtype=np.float32)
                 * np.float32(-math.log(10000.0) / d_pos))
    ang = (pos * div).astype(np.float32)
    pe = np.zeros((seq_len, d_pos), dtype=np.float32)
    pe[:, 0::2] = np.sin(ang)
    pe[:, 1::2] = np.cos(ang)
    return pe


class Ctr:
    def __init__(self, nc, name, step):
        self.h = nc.alloc_semaphore(name)
        self.n = 0
        self.step = step

    def tick(self):
        self.n += self.step
        return self.n


class Em:
    def __init__(self, nc):
        self.nc = nc
        self.steps = {e: [] for e in ENGINES}
        self.waited = {e: {} for e in ENGINES}

    def wait(self, e, ctr, val):
        if val is None or val <= 0:
            return
        if self.waited[e].get(ctr, 0) >= val:
            return
        self.waited[e][ctr] = val
        self.steps[e].append(lambda eng, h=ctr.h, v=val: eng.wait_ge(h, v))

    def op(self, e, fn, inc=None):
        if inc is None:
            self.steps[e].append(fn)
            return None
        tok = inc.tick()
        self.steps[e].append(lambda eng, f=fn, h=inc.h, s=inc.step: f(eng).then_inc(h, s))
        return tok


def build_program(NT=32, DBG=(), STOP=None, SKIP=()):
    from contextlib import ExitStack
    nc = bass.Bass("TRN2", target_bir_lowering=False)
    NTOK = NT * 128
    NRT = NT + 7
    NTS = NRT * 128

    x_in = nc.dram_tensor("x_in", [NTOK, D], F32, kind="ExternalInput").ap()
    wkT = nc.dram_tensor("wkT", [128, NDC, D], F32, kind="ExternalInput").ap()
    wvT = nc.dram_tensor("wvT", [128, NDC, D], F32, kind="ExternalInput").ap()
    wqT = nc.dram_tensor("wqT", [128, NDC, D], F32, kind="ExternalInput").ap()
    wfT = nc.dram_tensor("wfT", [8, 128, NDC, D], F32, kind="ExternalInput").ap()
    bk_row = nc.dram_tensor("bk_row", [1, D], F32, kind="ExternalInput").ap()
    bv_row = nc.dram_tensor("bv_row", [1, D], F32, kind="ExternalInput").ap()
    bq_col = nc.dram_tensor("bq_col", [128, 8], F32, kind="ExternalInput").ap()
    g_row = nc.dram_tensor("g_row", [1, D], F32, kind="ExternalInput").ap()
    be_row = nc.dram_tensor("be_row", [1, D], F32, kind="ExternalInput").ap()
    bffn = nc.dram_tensor("bffn", [8, D], F32, kind="ExternalInput").ap()
    sigsT = nc.dram_tensor("sigsT", [32, 8], F32, kind="ExternalInput").ap()
    peJ = nc.dram_tensor("peJ", [32, NT, 128], F32, kind="ExternalInput").ap()
    ltri = nc.dram_tensor("ltri", [128, 129], BF16, kind="ExternalInput").ap()
    ltri8 = nc.dram_tensor("ltri8", [8, 9], F32, kind="ExternalInput").ap()
    iota_pj = nc.dram_tensor("iota_pj", [128, NT], F32, kind="ExternalInput").ap()
    i128row = nc.dram_tensor("i128row", [1, NRT], F32, kind="ExternalInput").ap()
    thr832 = nc.dram_tensor("thr832", [8, 32], F32, kind="ExternalInput").ap()
    iota_p8 = nc.dram_tensor("iota_p8", [128, 1], F32, kind="ExternalInput").ap()
    identf = nc.dram_tensor("identf", [128, 128], F32, kind="ExternalInput").ap()
    identb = nc.dram_tensor("identb", [128, 128], BF16, kind="ExternalInput").ap()
    y_out = nc.dram_tensor("y_out", [NTOK, D], F32, kind="ExternalOutput").ap()
    g_dram = nc.dram_tensor("g_dram", [64, 128], F32, kind="Internal").ap()
    m_dram = nc.dram_tensor("m_dram", [64, 128], F32, kind="Internal").ap()
    guard = nc.dram_tensor("dbg_guard", [128, 512], F32, kind="ExternalOutput").ap() if "guard" in DBG else None
    xn_dram = nc.dram_tensor("xn_dram", [NTS, D], BF16, kind="Internal").ap()
    q_dram = nc.dram_tensor("q_dram", [8, 128, NTS], BF16, kind="Internal").ap()
    wf_bf_flat = nc.dram_tensor("wf_bf", [8 * 128 * NDC, D], BF16, kind="Internal").ap()
    wf_bf = wf_bf_flat.rearrange("(t p dc) o -> t p dc o", t=8, p=128)
    bias_bf = nc.dram_tensor("bias_bf", [8, D], BF16, kind="Internal").ap()

    dbg_out = {}
    for name, shape, dt_ in (("f", [128, NT, 8], F32), ("dest", [128, NT], F32),
                             ("g", [128, 64], F32), ("tid", [1, NRT], F32),
                             ("kv", [128, 1536], F32), ("widx", [128, NRT], F32),
                             ("k0", [128, D], BF16), ("xn0", [128, D], BF16), ("msk", [128, 64], F32),
                             ("xn20", [128, D], BF16), ("qb0", [128, 512], BF16), ("y0", [128, D], F32),
                             ("xnt0", [128, 256], BF16)):
        if name in DBG:
            dbg_out[name] = nc.dram_tensor("dbg_" + name, shape, dt_,
                                           kind="ExternalOutput").ap()

    em = Em(nc)
    dX = Ctr(nc, "dX", 16)
    dG = Ctr(nc, "dG", 16)
    dT = Ctr(nc, "dT", 16)
    dO = Ctr(nc, "dO", 16)
    dQ = Ctr(nc, "dQ", 16)
    dGW0 = Ctr(nc, "dGW0", 16)
    dGW1 = Ctr(nc, "dGW1", 16)
    dX2a = Ctr(nc, "dX2a", 16)
    dX2b = Ctr(nc, "dX2b", 16)
    dXNa = Ctr(nc, "dXNa", 16)
    dXNb = Ctr(nc, "dXNb", 16)
    dB = Ctr(nc, "dB", 16)
    dGX = Ctr(nc, "dGX", 16)
    dWT = Ctr(nc, "dWT", 16)
    pP = Ctr(nc, "pP", 1)
    pA = Ctr(nc, "pA", 1)
    pV = Ctr(nc, "pV", 1)

    es = ExitStack()
    sb = lambda name, shape, dtype=F32: es.enter_context(nc.sbuf_tensor(name, shape, dtype))[:]

    def finish():
        em.wait("sync", dX, dX.n)
        em.wait("sync", dO, dO.n)
        em.wait("sync", dQ, dQ.n)
        em.wait("gpsimd", dG, dG.n)
        em.wait("gpsimd", dGW0, dGW0.n)
        em.wait("gpsimd", dGW1, dGW1.n)
        em.wait("gpsimd", dX2a, dX2a.n)
        em.wait("gpsimd", dX2b, dX2b.n)
        em.wait("gpsimd", dB, dB.n)
        em.wait("gpsimd", dGX, dGX.n)
        em.wait("gpsimd", dWT, dWT.n)
        em.wait("scalar", dT, dT.n)
        em.wait("tensor", pP, pP.n)
        em.wait("vector", pV, pV.n)
        em.wait("scalar", pA, pA.n)
        with nc.Block() as block:
            for ename in ENGINES:
                getattr(block, ename)(
                    lambda eng, el=tuple(em.steps[ename]): [fn(eng) for fn in el])
        es.close()
        return nc

    xnT = [sb(f"xnT{dc}", [128, NTS], BF16) for dc in range(NDC)]
    kvblk = sb("kvblk", [128, NG, 130], BF16)
    gamma_b = sb("gamma_b", [128, D])
    beta_b = sb("beta_b", [128, D])
    ones1f = sb("ones1f", [1, 128])
    ones1b = sb("ones1b", [1, 128], BF16)
    bkr_sb = sb("bkr_sb", [1, D], BF16)
    ber_bf = sb("ber_bf", [1, D], BF16)
    bvr_sb = sb("bvr_sb", [1, D], BF16)
    bqc_sb = sb("bqc_sb", [128, 8])
    bffn_sb = sb("bffn_sb", [8, D], BF16)
    sigs_sb = sb("sigs_sb", [32, 8])
    sgn_sb = sb("sgn_sb", [32, 8])
    eps_sb = sb("eps_sb", [128, 1])
    g_sb = sb("g_sb", [128, 64])
    gu_sb = sb("gu_sb", [128, 64], U32)
    gsafe_sb = sb("gsafe_sb", [128, 64])
    gusafe_sb = sb("gusafe_sb", [128, 64], U32)
    msk_sb = sb("msk_sb", [128, 64])
    tid_sb = sb("tid_sb", [1, NRT])
    tidu_sb = sb("tidu_sb", [1, NRT], U32)
    widx_sb = sb("widx_sb", [128, NRT])
    widxu_sb = sb("widxu_sb", [128, NRT], U32)
    tidbu_sb = sb("tidbu_sb", [128, NRT], U32)
    iota8_sb = sb("iota8_sb", [128, 1])
    identb_sb = sb("identb_sb", [128, 128], BF16)
    kvdbg = sb("kvdbg", [128, 1536]) if "kv" in DBG else None
    T = {}
    from contextlib import ExitStack as _ES
    es0p = _ES()
    if True:
        _peJ = es0p.enter_context(nc.sbuf_tensor("peJ_sb", [32, NT * 128], F32))
        _grow = es0p.enter_context(nc.sbuf_tensor("grow_sb", [1, D], F32))
        _berow = es0p.enter_context(nc.sbuf_tensor("berow_sb", [1, D], F32))
        _ltri = es0p.enter_context(nc.sbuf_tensor("ltri_sb", [128, 129], BF16))
        _ltri8 = es0p.enter_context(nc.sbuf_tensor("ltri8_sb", [8, 9], F32))
        _iotaJ = es0p.enter_context(nc.sbuf_tensor("iotaJ_sb", [128, NT], F32))
        _i128 = es0p.enter_context(nc.sbuf_tensor("i128_sb", [1, 64], F32))
        _thr = es0p.enter_context(nc.sbuf_tensor("thr_sb", [8, 32], F32))
        _identf = es0p.enter_context(nc.sbuf_tensor("identf_sb", [128, 128], F32))
        _scores = es0p.enter_context(nc.sbuf_tensor("scores_sb", [128, NT, 8], F32))
        _m = es0p.enter_context(nc.sbuf_tensor("m_sb", [128, NT], F32))
        _eq = es0p.enter_context(nc.sbuf_tensor("eq_sb", [128, NT], F32))
        _noty = es0p.enter_context(nc.sbuf_tensor("noty_sb", [128, NT], F32))
        _f = es0p.enter_context(nc.sbuf_tensor("f_sb", [128, NT, 8], F32))
        _c8 = es0p.enter_context(nc.sbuf_tensor("c8_sb", [128, 8], F32))
        _cbf = es0p.enter_context(nc.sbuf_tensor("cbf_sb", [128, 8], BF16))
        _pt = es0p.enter_context(nc.sbuf_tensor("pt_sb", [8, 129], F32))
        _small = es0p.enter_context(nc.sbuf_tensor("small_sb", [8, 40], F32))
        _rows = es0p.enter_context(nc.sbuf_tensor("rows_sb", [1, 20], F32))
        _ptm = es0p.enter_context(nc.sbuf_tensor("ptm_sb", [128, 8], F32))
        _padb = es0p.enter_context(nc.sbuf_tensor("padb_sb", [128, 8], F32))
        _pp = es0p.enter_context(nc.sbuf_tensor("pp_sb", [128, 8], F32))
        _w = es0p.enter_context(nc.sbuf_tensor("w_sb", [128, NT, 8], F32))
        _w3 = es0p.enter_context(nc.sbuf_tensor("w3_sb", [128, NT, 8], F32))
        _dest = es0p.enter_context(nc.sbuf_tensor("dest_sb", [128, NT], F32))
        _destu = es0p.enter_context(nc.sbuf_tensor("destu_sb", [128, NT], U32))
        _sent = es0p.enter_context(nc.sbuf_tensor("sent_sb", [64, 128], F32))
        _zero = es0p.enter_context(nc.sbuf_tensor("zero_sb", [64, 128], F32))
        _onespj = es0p.enter_context(nc.sbuf_tensor("onespj_sb", [128, NT], F32))
        _tmpr = es0p.enter_context(nc.sbuf_tensor("tmpr_sb", [1, 64], F32))
        _gld = es0p.enter_context(nc.sbuf_tensor("gld_sb", [64, 128], F32))
        _mld = es0p.enter_context(nc.sbuf_tensor("mld_sb", [64, 128], F32))
        _sc_ps = es0p.enter_context(nc.psum_tensor("sc_ps", [128, NT * 8], F32))
        _pt_ps = es0p.enter_context(nc.psum_tensor("pt_ps", [8, 129], F32))
        _pad_ps = es0p.enter_context(nc.psum_tensor("pad_ps", [1, 9], F32))
        _padb_ps = es0p.enter_context(nc.psum_tensor("padb_ps", [128, 8], F32))
        _wix_ps = es0p.enter_context(nc.psum_tensor("wix_ps", [128, 64], F32))
        _tr_ps = es0p.enter_context(nc.psum_tensor("tr_ps", [128, 128], F32))
        _gb_ps = es0p.enter_context(nc.psum_tensor("gb_ps", [128, D], F32))
        peJ_sb, grow_sb, berow_sb = _peJ[:], _grow[:], _berow[:]
        ltri_s, ltri8_s, iotaJ = _ltri[:], _ltri8[:], _iotaJ[:]
        i128, thr_s, idf = _i128[:][:, 0:NRT], _thr[:], _identf[:]
        scores_sb, m_sb, eq_sb, noty_sb, f_sb = _scores[:], _m[:], _eq[:], _noty[:], _f[:]
        c8_sb, cbf_sb, pt_sb = _c8[:], _cbf[:], _pt[:]
        ntc, thrcnt, padc = _small[:][:, 0:1], _small[:][:, 2:34], _small[:][:, 36:37]
        padrow, inclrow = _rows[:][:, 0:9], _rows[:][:, 10:18]
        ptm, padb, pp_sb = _ptm[:], _padb[:], _pp[:]
        w_sb, w3_sb, dest_sb, destu = _w[:], _w3[:], _dest[:], _destu[:]
        sent_sb, zero_sb, onespj = _sent[:], _zero[:], _onespj[:]
        tmpr = _tmpr[:][:, 0:NRT]
        gld_sb, mld_sb = _gld[:], _mld[:]
        sc_ps, pt_ps, pad_ps, padb_ps, wix_ps, tr_ps, gb_ps = (
            _sc_ps[:], _pt_ps[:], _pad_ps[:], _padb_ps[:], _wix_ps[:][:, 0:NRT],
            _tr_ps[:], _gb_ps[:])

        for dst, src in ((peJ_sb, peJ.rearrange("d j p -> d (j p)")), (grow_sb, g_row),
                         (berow_sb, be_row), (ltri_s, ltri), (ltri8_s, ltri8),
                         (iotaJ, iota_pj), (_i128[:][:, 0:NRT], i128row), (thr_s, thr832),
                         (idf, identf), (bqc_sb, bq_col), (sigs_sb, sigsT),
                         (iota8_sb, iota_p8), (identb_sb, identb)):
            t_ld = em.op("sync", lambda e, d=dst, s=src: e.dma_start(out=d, in_=s), dX)
        T[("lds",)] = t_ld
        t_bffn = em.op("gpsimd", lambda e: e.dma_start(out=bffn_sb, in_=bffn), dG)
        t_bkr = em.op("gpsimd", lambda e: e.dma_start(out=bkr_sb, in_=bk_row), dG)
        T[("berbf",)] = em.op("gpsimd", lambda e: e.dma_start(out=ber_bf, in_=be_row), dG)
        t_bvr = em.op("gpsimd", lambda e: e.dma_start(out=bvr_sb, in_=bv_row), dG)
        T[("bkr",)] = t_bkr
        T[("bvr",)] = t_bvr
        em.wait("gpsimd", dG, t_bffn)
        T[("bias_bf",)] = em.op("gpsimd", lambda e: e.dma_start(out=bias_bf, in_=bffn_sb), dG)

        for ap_, val in ((ones1f, 1.0), (ones1b, 1.0), (eps_sb, EPS_LN),
                         (noty_sb, 1.0), (onespj, 1.0), (sent_sb, SENT),
                         (zero_sb, 0.0), (tid_sb, 0.0)):
            t_ms = em.op("vector", lambda e, a=ap_, v=val: e.memset(a, v), pV)
        T[("ms",)] = t_ms

        em.wait("tensor", pV, T[("ms",)])
        em.wait("tensor", dX, T[("lds",)])
        em.op("tensor", lambda e: e.matmul(gb_ps[:, 0:512], ones1f, grow_sb[:, 0:512],
                                           start=True, stop=True))
        t_g = em.op("tensor", lambda e: e.matmul(gb_ps[:, 512:1024], ones1f,
                                                 grow_sb[:, 512:1024], start=True, stop=True), pP)
        em.wait("vector", pP, t_g)
        t_gev = em.op("vector", lambda e: e.tensor_copy(gamma_b, gb_ps), pV)
        em.wait("tensor", pV, t_gev)
        em.op("tensor", lambda e: e.matmul(gb_ps[:, 0:512], ones1f, berow_sb[:, 0:512],
                                           start=True, stop=True))
        t_b = em.op("tensor", lambda e: e.matmul(gb_ps[:, 512:1024], ones1f,
                                                 berow_sb[:, 512:1024], start=True, stop=True), pP)
        em.wait("vector", pP, t_b)
        T[("gb",)] = em.op("vector", lambda e: e.tensor_copy(beta_b, gb_ps), pV)

        em.wait("scalar", dX, T[("lds",)])
        T[("sgn",)] = em.op("scalar", lambda e: e.activation(sgn_sb, sigs_sb, AF.Sign), pA)
        em.wait("tensor", pA, T[("sgn",)])
        for j in range(NT):
            t_sc = em.op("tensor", lambda e, j=j: e.matmul(
                sc_ps[:, j * 8:(j + 1) * 8], peJ_sb[:, j * 128:(j + 1) * 128],
                sgn_sb, start=(j == 0), stop=(j == NT - 1)), pP)
        em.wait("vector", pP, t_sc)
        t = em.op("vector", lambda e: e.tensor_copy(scores_sb, sc_ps), pV)
        em.wait("vector", pV, t)
        t = em.op("vector", lambda e: e.reduce_max(m_sb, scores_sb, axis=AX.X), pV)
        for t8 in range(8):
            em.wait("vector", pV, t)
            t = em.op("vector", lambda e, t8=t8: e.tensor_tensor(
                eq_sb, scores_sb[:, :, t8], m_sb, ALU.is_equal), pV)
            em.wait("vector", pV, t)
            t = em.op("vector", lambda e, t8=t8: e.tensor_tensor(
                f_sb[:, :, t8], eq_sb, noty_sb, ALU.mult), pV)
            em.wait("vector", pV, t)
            t = em.op("vector", lambda e, t8=t8: e.tensor_tensor(
                noty_sb, noty_sb, f_sb[:, :, t8], ALU.subtract), pV)
        T[("f",)] = t
        em.wait("vector", pV, t)
        t = em.op("vector", lambda e: e.reduce_sum(
            c8_sb, f_sb.rearrange("p a b -> p b a"), axis=AX.X), pV)
        em.wait("vector", pV, t)
        t_cbf = em.op("vector", lambda e: e.tensor_copy(cbf_sb, c8_sb), pV)
        em.wait("tensor", pV, t_cbf)
        t_pt = em.op("tensor", lambda e: e.matmul(pt_ps, cbf_sb, ltri_s,
                                                  start=True, stop=True), pP)
        em.wait("vector", pP, t_pt)
        t_ptev = em.op("vector", lambda e: e.tensor_copy(pt_sb, pt_ps), pV)
        em.wait("vector", pV, t_ptev)
        t = em.op("vector", lambda e: e.tensor_copy(ntc, pt_sb[:, 128:129]), pV)
        em.wait("vector", pV, t)
        t = em.op("vector", lambda e: e.tensor_scalar(thrcnt, thr_s, ntc, None, ALU.is_lt), pV)
        em.wait("vector", pV, t)
        t = em.op("vector", lambda e: e.reduce_sum(padc, thrcnt, axis=AX.X), pV)
        em.wait("vector", pV, t)
        t_ceil = em.op("vector", lambda e: e.tensor_scalar(padc, padc, 128.0, None, ALU.mult), pV)
        em.wait("tensor", pV, t_ceil)
        t_pad = em.op("tensor", lambda e: e.matmul(pad_ps, padc, ltri8_s,
                                                   start=True, stop=True), pP)
        em.wait("tensor", pV, t_ptev)
        t_tr1 = em.op("tensor", lambda e: e.transpose(tr_ps[:, 0:8], pt_sb[:, 0:128],
                                                      idf[0:8, 0:8]), pP)
        em.wait("vector", pP, t_pad)
        t_prow = em.op("vector", lambda e: e.tensor_copy(padrow, pad_ps), pV)
        em.wait("vector", pP, t_tr1)
        t_ptm = em.op("vector", lambda e: e.tensor_copy(ptm, tr_ps[:, 0:8]), pV)
        em.wait("vector", pV, t_prow)
        t_incl = em.op("vector", lambda e: e.tensor_copy(inclrow, padrow[:, 1:9]), pV)
        em.wait("tensor", pV, t_prow)
        t_pb = em.op("tensor", lambda e: e.matmul(padb_ps, ones1f, padrow[:, 0:8],
                                                  start=True, stop=True), pP)
        em.wait("vector", pP, t_pb)
        t = em.op("vector", lambda e: e.tensor_copy(padb, padb_ps), pV)
        em.wait("vector", pV, t)
        em.wait("vector", pV, t_ptm)
        t_pp = em.op("vector", lambda e: e.tensor_tensor(pp_sb, ptm, padb, ALU.add), pV)
        em.wait("vector", pV, T[("f",)])
        for t8 in range(8):
            t = em.op("vector", lambda e, t8=t8: e.tensor_tensor_scan(
                w_sb[:, :, t8], f_sb[:, :, t8], f_sb[:, :, t8], 0.0, ALU.add, ALU.bypass), pV)
        em.wait("vector", pV, t)
        t = em.op("vector", lambda e: e.tensor_tensor(w_sb[:], w_sb[:], f_sb[:], ALU.subtract), pV)
        em.wait("vector", pV, t)
        em.wait("vector", pV, t_pp)
        ppb = pp_sb.rearrange("p t -> p () t").broadcast_to([128, NT, 8])
        t = em.op("vector", lambda e, ppb=ppb: e.tensor_tensor(w3_sb[:], w_sb[:], ppb, ALU.add), pV)
        em.wait("vector", pV, t)
        t = em.op("vector", lambda e: e.tensor_tensor(w3_sb[:], w3_sb[:], f_sb[:], ALU.mult), pV)
        em.wait("vector", pV, t)
        t = em.op("vector", lambda e: e.reduce_sum(dest_sb, w3_sb[:], axis=AX.X), pV)
        em.wait("vector", pV, t)
        T[("destu",)] = em.op("vector", lambda e: e.tensor_copy(destu, dest_sb), pV)

        em.wait("vector", pV, t_incl)
        t = T[("ms",)]
        for t8 in range(7):
            em.wait("vector", pV, t)
            t = em.op("vector", lambda e, t8=t8: e.tensor_scalar(
                tmpr, i128, inclrow[:, t8:t8 + 1], None, ALU.is_ge), pV)
            em.wait("vector", pV, t)
            t = em.op("vector", lambda e: e.tensor_tensor(tid_sb, tid_sb, tmpr, ALU.add), pV)
        em.wait("vector", pV, t)
        T[("tidu",)] = em.op("vector", lambda e: e.tensor_copy(tidu_sb, tid_sb), pV)
        em.wait("vector", pV, t)
        t_t1k = em.op("vector", lambda e: e.tensor_scalar(tmpr, tid_sb, 1024.0, None, ALU.mult), pV)
        em.wait("tensor", pV, t_t1k)
        t_wb = em.op("tensor", lambda e: e.matmul(wix_ps, ones1f, tmpr,
                                                  start=True, stop=True), pP)
        em.wait("vector", pP, t_wb)
        em.wait("vector", dX, T[("lds",)])
        t = em.op("vector", lambda e: e.tensor_scalar(widx_sb, wix_ps, iota8_sb, None, ALU.add), pV)
        em.wait("vector", pV, t)
        T[("widx",)] = em.op("vector", lambda e: e.tensor_copy(widxu_sb, widx_sb), pV)
        em.wait("tensor", pV, T[("widx",)])
        em.wait("tensor", pV, T[("tidu",)])
        t_tb = em.op("tensor", lambda e: e.matmul(wix_ps, ones1f, tid_sb,
                                                  start=True, stop=True), pP)
        em.wait("vector", pP, t_tb)
        T[("tidb",)] = em.op("vector", lambda e: e.tensor_copy(tidbu_sb, wix_ps), pV)

        em.wait("gpsimd", pV, T[("ms",)])
        t_f2 = em.op("gpsimd", lambda e: e.dma_start(out=g_dram, in_=sent_sb), dG)
        em.wait("gpsimd", dG, t_f2)
        em.wait("gpsimd", pV, T[("destu",)])
        gflat = g_dram.rearrange("a b -> (a b) ()")
        mflat = m_dram.rearrange("a b -> (a b) ()")
        for j in range(NT):
            t_sc2 = em.op("gpsimd", lambda e, j=j: e.indirect_dma_start(
                out=gflat, out_offset=IndirectOffsetOnAxis(destu[:, j:j + 1], 0),
                in_=iotaJ[:, j:j + 1], in_offset=None), dG)
        em.wait("gpsimd", dG, t_sc2)
        t_mld = em.op("gpsimd", lambda e: e.dma_start(out=gld_sb, in_=g_dram), dG)
        em.wait("tensor", dG, t_mld)
        em.wait("tensor", pV, t_ptm)
        t_tr2 = em.op("tensor", lambda e: e.transpose(tr_ps[:, 0:64], gld_sb,
                                                      idf[0:64, 0:64]), pP)
        em.wait("vector", pP, t_tr2)
        t_gsb = em.op("vector", lambda e: e.tensor_copy(g_sb, tr_ps[:, 0:64]), pV)
        em.wait("vector", pV, t_gsb)
        T[("g",)] = em.op("vector", lambda e: e.tensor_copy(gu_sb, g_sb), pV)
        t_gs = em.op("vector", lambda e: e.tensor_scalar(
            gsafe_sb, g_sb, float(NTOK - 1), None, ALU.min), pV)
        em.wait("vector", pV, t_gs)
        T[("gsafe",)] = em.op("vector", lambda e: e.tensor_copy(gusafe_sb, gsafe_sb), pV)
        em.wait("vector", pV, t_gsb)
        T[("msk",)] = em.op("vector", lambda e: e.tensor_scalar(
            msk_sb, g_sb, float(SENT), None, ALU.is_lt), pV)

        if "f" in DBG:
            em.wait("sync", pV, T[("f",)])
            em.op("sync", lambda e, d=dbg_out["f"]: e.dma_start(out=d, in_=f_sb[:]), dO)
        if "dest" in DBG:
            em.wait("sync", pV, T[("destu",)])
            em.op("sync", lambda e, d=dbg_out["dest"]: e.dma_start(out=d, in_=dest_sb), dO)
        if "g" in DBG:
            em.wait("sync", pV, T[("g",)])
            em.op("sync", lambda e, d=dbg_out["g"]: e.dma_start(out=d, in_=g_sb), dO)
        if "tid" in DBG:
            em.wait("sync", pV, T[("tidu",)])
            em.op("sync", lambda e, d=dbg_out["tid"]: e.dma_start(out=d, in_=tid_sb), dO)
        if "widx" in DBG:
            em.wait("sync", pV, T[("widx",)])
            em.op("sync", lambda e, d=dbg_out["widx"]: e.dma_start(out=d, in_=widx_sb), dO)
        if "msk" in DBG:
            em.wait("sync", pV, T[("msk",)])
            em.op("sync", lambda e, d=dbg_out["msk"]: e.dma_start(out=d, in_=msk_sb), dO)

        ph0 = dict(pP=pP.n, pA=pA.n, pV=pV.n, dX=dX.n, dG=dG.n, dO=dO.n)
    es0p.close()
    if STOP == 0:
        return finish()
    with nc.sbuf_tensor("wk_sb", [128, NDC * D], BF16) as _wk, \
         nc.sbuf_tensor("wv_sb", [128, NDC * D], BF16) as _wv, \
         nc.sbuf_tensor("wtmp_sb", [128, 2, 2 * D], BF16) as _wtmp, \
         nc.sbuf_tensor("x_sb", [128, 2, D], F32) as _x, \
         nc.sbuf_tensor("xnbf", [128, 3, D], BF16) as _xnbf, \
         nc.sbuf_tensor("sq1_scr", [128, D], BF16) as _sq1, \
         nc.sbuf_tensor("st1", [128, 16], F32) as _st1, \
         nc.sbuf_tensor("k_bf", [128, D], BF16) as _kbf, \
         nc.sbuf_tensor("v_aug", [128, NG, 130], BF16) as _vaug, \
         nc.psum_tensor("kv_ps", [128, 1536], F32) as _kv_ps, \
         nc.psum_tensor("k_ps", [128, D], F32) as _k_ps, \
         nc.psum_tensor("v_ps", [128, D], F32) as _v_ps:
        wk_sb = _wk[:].rearrange("p (dc o) -> p dc o", dc=NDC)
        wv_sb = _wv[:].rearrange("p (dc o) -> p dc o", dc=NDC)
        wtmp, x_sb, xnbf, sq1_scr, st1 = _wtmp[:], _x[:], _xnbf[:], _sq1[:], _st1[:]
        k_bf, v_aug = _kbf[:], _vaug[:]
        kv_ps, k_ps, v_ps = _kv_ps[:], _k_ps[:], _v_ps[:]
        stc = lambda idx, pb: st1[:, 2 * idx + pb:2 * idx + pb + 1]
        kvoff = lambda g: (g // 3) * 512 + (g % 3) * 130

        for ctr, key in ((pP, "pP"), (pA, "pA"), (pV, "pV"), (dX, "dX"), (dO, "dO")):
            em.wait("gpsimd", ctr, ph0[key])
        em.wait("gpsimd", dG, ph0["dG"])
        t_wk = em.op("gpsimd", lambda e: e.dma_start(out=wk_sb, in_=wkT), dWT)
        t_wv = em.op("gpsimd", lambda e: e.dma_start(out=wv_sb, in_=wvT), dWT)
        em.wait("vector", pV, T[("ms",)])
        t_vone = em.op("vector", lambda e: e.memset(v_aug, 1.0), pV)
        t_kvz = em.op("vector", lambda e: e.memset(kvblk, 0.0), pV)
        t_xz = em.op("vector", lambda e: e.memset(x_sb, 0.0), pV)
        em.wait("gpsimd", pV, t_xz)

        def gather_fe(ig):
            if ig >= 2:
                em.wait("gpsimd", pA, T[("sq", ig - 2)])
                em.wait("gpsimd", pV, T[("xn", ig - 2)])
            em.wait("gpsimd", pV, T[("gsafe",)])
            if ig >= 1:
                em.wait("gpsimd", dGX, T[("xg", ig - 1)])
            T[("xg", ig)] = em.op("gpsimd", lambda e, ig=ig, pg=ig % 2: e.indirect_dma_start(
                out=x_sb[:, pg, :], out_offset=None,
                in_=x_in, in_offset=IndirectOffsetOnAxis(gusafe_sb[:, ig:ig + 1], 0)), dGX)

        def norm_fe(ig):
            pg = ig % 2
            em.wait("scalar", dGX, T[("xg", ig)])
            T[("sq", ig)] = em.op("scalar", lambda e, pg=pg: e.activation(
                sq1_scr, x_sb[:, pg, :], AF.Square, accum_out=stc(0, pg)), pA)
            em.wait("vector", dGX, T[("xg", ig)])
            t = em.op("vector", lambda e, pg=pg: e.reduce_sum(stc(1, pg), x_sb[:, pg, :], axis=AX.X), pV)
            em.wait("vector", pV, t)
            t = em.op("vector", lambda e, pg=pg: e.tensor_scalar(stc(2, pg), stc(1, pg), float(RD), None, ALU.mult), pV)
            em.wait("vector", pV, t)
            t = em.op("vector", lambda e, pg=pg: e.tensor_tensor(stc(3, pg), stc(2, pg), stc(2, pg), ALU.mult), pV)
            em.wait("vector", pA, T[("sq", ig)])
            t2 = em.op("vector", lambda e, pg=pg: e.tensor_scalar(stc(4, pg), stc(0, pg), float(RD), None, ALU.mult), pV)
            em.wait("vector", pV, t2)
            t3 = em.op("vector", lambda e, pg=pg: e.tensor_tensor(stc(5, pg), stc(4, pg), stc(3, pg), ALU.subtract), pV)
            em.wait("scalar", pV, t3)
            T[("sd", ig)] = em.op("scalar", lambda e, pg=pg: e.activation(
                stc(6, pg), stc(5, pg), AF.Sqrt, bias=eps_sb), pA)
            em.wait("vector", pA, T[("sd", ig)])
            t4 = em.op("vector", lambda e, pg=pg: e.reciprocal(stc(7, pg), stc(6, pg)), pV)
            em.wait("vector", pV, t4)
            if ig >= 3:
                em.wait("vector", dT, T[("trs", ig - 3)])
                em.wait("vector", dO, T[("spill", ig - 3)])
            for dc in range(NDC):
                t_xn = em.op("vector", lambda e, pg=pg, p3=ig % 3, dc=dc: e.tensor_scalar(
                    xnbf[:, p3, dc * 128:(dc + 1) * 128],
                    x_sb[:, pg, dc * 128:(dc + 1) * 128], stc(2, pg), stc(7, pg),
                    ALU.subtract, ALU.mult), pV)
                T[("xnc", ig, dc)] = t_xn
            T[("xn", ig)] = t_xn

            em.wait("scalar", pV, T[("xn", ig)])
            if ig >= 1:
                em.wait("scalar", dO, T[("spill", ig - 1)])
            T[("spill", ig)] = em.op("scalar", lambda e, ig=ig, p3=ig % 3: e.dma_start(
                out=xn_dram[ig * 128:(ig + 1) * 128, :], in_=xnbf[:, p3, :]), dO)
            for dc in range(NDC):
                em.wait("sync", pV, T[("xnc", ig, dc)])
                trs = em.op("sync", lambda e, dc=dc, p3=ig % 3, ig=ig: e.dma_start(
                    out=xnT[dc][:, ig * 128:(ig + 1) * 128],
                    in_=xnbf[:, p3, dc * 128:(dc + 1) * 128], transpose=True), dT)
                T[("trs", ig, dc)] = trs
            T[("trs", ig)] = trs

        for ig in range(min(2, NRT)):
            gather_fe(ig)
        if NRT >= 1:
            norm_fe(0)
        for i in range(NRT):
            pb = i % 2
            if i + 2 < NRT:
                gather_fe(i + 2)
            if i + 1 < NRT:
                norm_fe(i + 1)
            cpt = -(-32 // NRT)
            for ck in range(i * cpt, min((i + 1) * cpt, 32)):
                t8, ci = ck // 4, ck % 4
                wb_ = ck % 2
                if ck >= 1:
                    em.wait("gpsimd", dG, T[("wfst", ck - 1)])
                T[("wfld", ck)] = em.op("gpsimd", lambda e, t8=t8, ci=ci, wb_=wb_: e.dma_start(
                    out=wtmp[:, wb_, :].rearrange("p (a o) -> p a o", a=2),
                    in_=wfT[t8, :, 2 * ci:2 * ci + 2, :]), dG)
                em.wait("gpsimd", dG, T[("wfld", ck)])
                T[("wfst", ck)] = em.op("gpsimd", lambda e, t8=t8, ci=ci, wb_=wb_: e.dma_start(
                    out=wf_bf[t8, :, 2 * ci:2 * ci + 2, :],
                    in_=wtmp[:, wb_, :].rearrange("p (a o) -> p a o", a=2)), dG)

            em.wait("tensor", pV, ph0["pV"])
            em.wait("tensor", dWT, t_wk)
            em.wait("tensor", dG, T[("bkr",)])
            em.wait("tensor", pV, T[("ms",)])
            if i >= 1:
                em.wait("tensor", pA, T[("krel", i - 1)])
            for dc in range(NDC):
                em.wait("tensor", dT, T[("trs", i, dc)])
                for hf in range(2):
                    sl = slice(hf * 512, hf * 512 + 512)
                    em.op("tensor", lambda e, dc=dc, sl=sl, i=i: e.matmul(
                        k_ps[:, sl], xnT[dc][:, i * 128:(i + 1) * 128],
                        wk_sb[:, dc, sl], start=(dc == 0), stop=False))
            for hf in range(2):
                sl = slice(hf * 512, hf * 512 + 512)
                tkmm = em.op("tensor", lambda e, sl=sl: e.matmul(
                    k_ps[:, sl], ones1b, bkr_sb[:, sl], start=False, stop=True), pP)
            T[("kmm", i)] = tkmm
            em.wait("scalar", pP, T[("kmm", i)])
            em.wait("scalar", pV, T[("msk",)])
            T[("krel", i)] = em.op("scalar", lambda e, i=i: e.activation(
                k_bf, k_ps, AF.Relu, scale=msk_sb[:, i:i + 1]), pA)
            em.wait("tensor", dWT, t_wv)
            em.wait("tensor", dG, T[("bvr",)])
            if i >= 1:
                em.wait("tensor", pV, T[("vaug", i - 1)])
            for dc in range(NDC):
                for hf in range(2):
                    sl = slice(hf * 512, hf * 512 + 512)
                    em.op("tensor", lambda e, dc=dc, sl=sl, i=i: e.matmul(
                        v_ps[:, sl], xnT[dc][:, i * 128:(i + 1) * 128],
                        wv_sb[:, dc, sl], start=(dc == 0), stop=False))
            for hf in range(2):
                sl = slice(hf * 512, hf * 512 + 512)
                tvmm = em.op("tensor", lambda e, sl=sl: e.matmul(
                    v_ps[:, sl], ones1b, bvr_sb[:, sl], start=False, stop=True), pP)
            T[("vmm", i)] = tvmm

            em.wait("vector", pP, T[("vmm", i)])
            em.wait("vector", pV, t_vone)
            em.wait("vector", pV, T[("msk",)])
            v3 = v_ps.rearrange("p (g e) -> p g e", g=NG)
            t_va = em.op("vector", lambda e, i=i, v3=v3: e.tensor_scalar(
                v_aug[:, :, 0:64], v3[:, :, 0:64], msk_sb[:, i:i + 1], None, ALU.mult), pV)
            em.wait("vector", pV, t_va)
            T[("vaug", i)] = em.op("vector", lambda e, i=i, v3=v3: e.tensor_scalar(
                v_aug[:, :, 65:129], v3[:, :, 64:128], msk_sb[:, i:i + 1], None, ALU.mult), pV)

            if i == 0 and "k0" in DBG:
                em.wait("sync", pA, T[("krel", 0)])
                em.op("sync", lambda e, d=dbg_out["k0"]: e.dma_start(out=d, in_=k_bf), dO)
            if i == 0 and "xn0" in DBG:
                em.wait("sync", pV, T[("xn", 0)])
                em.op("sync", lambda e, d=dbg_out["xn0"]: e.dma_start(out=d, in_=xnbf[:, 0, :]), dO)
            em.wait("tensor", pA, T[("krel", i)])
            em.wait("tensor", pV, T[("vaug", i)])
            for g in range(NG):
                st = (i == 0 and g % 3 == 0)
                sp = (i == NRT - 1 and (g % 3 == 2 or g == NG - 1))
                tkv = em.op("tensor", lambda e, g=g, st=st, sp=sp: e.matmul(
                    kv_ps[:, kvoff(g):kvoff(g) + 130],
                    k_bf[:, g * 128:(g + 1) * 128], v_aug[:, g, :],
                    start=st, stop=sp), pP)
            T[("kv", i)] = tkv

        if "xnt0" in DBG:
            em.wait("sync", pV, T[("tev", 0)])
            em.op("sync", lambda e, d=dbg_out["xnt0"]: e.dma_start(
                out=d, in_=xnT[0][:, 0:256]), dO)
        em.wait("vector", pP, T[("kv", NRT - 1)])
        em.wait("vector", pV, t_kvz)
        for g in range(NG):
            off = kvoff(g)
            em.op("vector", lambda e, g=g, off=off: e.tensor_copy(
                kvblk[0:64, g, 0:65], kv_ps[0:64, off:off + 65]))
            t_kvb = em.op("vector", lambda e, g=g, off=off: e.tensor_copy(
                kvblk[64:128, g, 65:130], kv_ps[64:128, off + 65:off + 130]), pV)
        T[("kvblk",)] = t_kvb
        if "kv" in DBG:
            t_kd = em.op("vector", lambda e: e.tensor_copy(kvdbg, kv_ps), pV)
            em.wait("sync", pV, t_kd)
            em.op("sync", lambda e, d=dbg_out["kv"]: e.dma_start(out=d, in_=kvdbg), dO)
        ph1 = dict(pP=pP.n, pA=pA.n, pV=pV.n, dT=dT.n, dX=dX.n, dG=dG.n, dO=dO.n)
    if STOP == 1:
        return finish()
    segs = []
    off = 0
    while off < NTS:
        n = min(2048, NTS - off)
        segs.append((off, n))
        off += n
    with nc.sbuf_tensor("wq_sb", [128, NDC * D], BF16) as _wq, \
         nc.sbuf_tensor("qseg", [128, 2, 2048], BF16) as _qseg, \
         nc.psum_tensor("q_ps", [128, 2, 2048], F32) as _q_ps:
        wq_sb = _wq[:].rearrange("p (dc o) -> p dc o", dc=NDC)
        qseg, q_ps = _qseg[:], _q_ps[:]
        for ctr, v in ((pP, ph1["pP"]), (pA, ph1["pA"]), (pV, ph1["pV"]),
                       (dT, ph1["dT"]), (dO, ph1["dO"]), (dX, ph1["dX"]),
                       (dGX, dGX.n), (dG, dG.n)):
            em.wait("gpsimd", ctr, v)
        t_wq = em.op("gpsimd", lambda e: e.dma_start(out=wq_sb, in_=wqT), dWT)
        em.wait("tensor", pV, T[("kvblk",)])
        em.wait("tensor", pA, T[("krel", NRT - 1)])
        em.wait("tensor", dWT, t_wq)
        nq = 0
        for si, (soff, sn) in enumerate(segs):
            for oc in range(8):
                sp = nq % 2
                if nq >= 2:
                    em.wait("tensor", pA, T[("qev", nq - 2)])
                for dc in range(NDC):
                    co = 0
                    while co < sn:
                        cn = min(512, sn - co)
                        tq = em.op("tensor", lambda e, oc=oc, dc=dc, sp=sp, soff=soff, co=co, cn=cn: e.matmul(
                            q_ps[:, sp, co:co + cn],
                            wq_sb[:, dc, oc * 128:(oc + 1) * 128],
                            xnT[dc][:, soff + co:soff + co + cn],
                            start=(dc == 0), stop=(dc == NDC - 1)), pP)
                        co += cn
                T[("qmm", nq)] = tq
                em.wait("scalar", pP, tq)
                if nq >= 2:
                    em.wait("scalar", dO, T[("qst", nq - 2)])
                T[("qev", nq)] = em.op("scalar", lambda e, oc=oc, sp=sp, sn=sn: e.activation(
                    qseg[:, sp, 0:sn], q_ps[:, sp, 0:sn], AF.Relu,
                    bias=bqc_sb[:, oc:oc + 1]), pA)
                em.wait("sync", pA, T[("qev", nq)])
                if nq >= 1:
                    em.wait("sync", dO, T[("qst", nq - 1)])
                T[("qst", nq)] = em.op("sync", lambda e, oc=oc, sp=sp, soff=soff, sn=sn: e.dma_start(
                    out=q_dram[oc][:, soff:soff + sn], in_=qseg[:, sp, 0:sn]), dO)
                nq += 1
        ph3 = dict(pP=pP.n, pA=pA.n, pV=pV.n, dO=dO.n, dG=dG.n)
    if STOP == 3:
        return finish()

    NBLK = 4
    with nc.sbuf_tensor("wsel", [128, 3, NDC * D], BF16) as _wsel, \
         nc.sbuf_tensor("bsel", [128, 3, D], BF16) as _bsel, \
         nc.sbuf_tensor("qblk", [128, 2, NG, 512], BF16) as _qblk, \
         nc.sbuf_tensor("x2_sb", [128, 3, D], F32) as _x2, \
         nc.sbuf_tensor("xn2", [128, 2, D], BF16) as _xn2, \
         nc.sbuf_tensor("y_sb", [128, D], F32) as _y, \
         nc.sbuf_tensor("o_sb", [128, 2, D], F32) as _o, \
         nc.sbuf_tensor("sq2_scr", [128, D], BF16) as _sq2, \
         nc.sbuf_tensor("dmx_sb", [128, 16], F32) as _dmx, \
         nc.sbuf_tensor("dinv_sb", [128, 16], F32) as _dinv, \
         nc.sbuf_tensor("st2", [128, 16], F32) as _st2, \
         nc.psum_tensor("ctx_ps", [128, 1536], F32) as _ctx_ps, \
         nc.psum_tensor("ffn_ps", [128, 2, D], F32) as _ffn_ps:
        wsel = _wsel[:].rearrange("p b (dc o) -> p b dc o", dc=NDC)
        wsel2d = _wsel[:]
        bsel, qblk = _bsel[:], _qblk[:]
        x2_sb, xn2, y_sb, o_sb, sq2_scr = _x2[:], _xn2[:], _y[:], _o[:], _sq2[:]
        dmx_sb, dinv_sb, st2 = _dmx[:], _dinv[:], _st2[:]
        ctx_ps, ffn_ps = _ctx_ps[:], _ffn_ps[:]
        coff = lambda g: (g // 3) * 512 + (g % 3) * 130
        s2 = lambda idx: st2[:, idx:idx + 1]
        wf_flat = wf_bf_flat

        ph3f = dict(pP=pP.n, pA=pA.n, pV=pV.n, dX=dX.n, dG=dG.n, dT=dT.n, dO=dO.n,
                    dGX=dGX.n, dWT=dWT.n)
        for eng in ("sync", "gpsimd"):
            for ctr, key in ((pP, "pP"), (pA, "pA"), (pV, "pV"), (dX, "dX"),
                             (dT, "dT"), (dO, "dO"), (dG, "dG"), (dGX, "dGX"),
                             (dWT, "dWT")):
                em.wait(eng, ctr, ph3f[key])
        em.wait("tensor", pA, ph3f["pA"])
        em.wait("tensor", pV, ph3f["pV"])

        def emit_gathers(ig):
            em.wait("gpsimd", pV, T[("widx",)])
            em.wait("gpsimd", pV, T[("tidu",)])
            em.wait("gpsimd", dG, T[("bias_bf",)])
            em.wait("gpsimd", dG, T[("wfst", 31)])
            if ig >= 3 and "ffn" not in SKIP:
                em.wait("gpsimd", pP, T[("fmm", ig - 3)])
            if "wg" not in SKIP:
                dgw = dGW0 if ig % 2 == 0 else dGW1
                if ig >= 2:
                    em.wait("gpsimd", dgw, T[("wg", ig - 2)][1])
                T[("wg", ig)] = (dgw, em.op("gpsimd", lambda e, ig=ig, wb=ig % 3: e.indirect_dma_start(
                    out=wsel2d[:, wb, :], out_offset=None,
                    in_=wf_flat, in_offset=IndirectOffsetOnAxis(widxu_sb[:, ig:ig + 1], 0)), dgw))
            em.wait("gpsimd", pV, T[("tidb",)])
            if ig >= 1:
                em.wait("gpsimd", dB, T[("bg", ig - 1)])
            T[("bg", ig)] = em.op("gpsimd", lambda e, ig=ig, wb=ig % 3: e.indirect_dma_start(
                out=bsel[:, wb, :], out_offset=None,
                in_=bias_bf, in_offset=IndirectOffsetOnAxis(tidbu_sb[:, ig:ig + 1], 0)), dB)
            if ig >= 3:
                em.wait("gpsimd", pV, T[("fin", ig - 3)])
            em.wait("gpsimd", pV, T[("gsafe",)])
            dx2 = dX2a if ig % 2 == 0 else dX2b
            if ig >= 2:
                em.wait("gpsimd", dx2, T[("x2", ig - 2)][1])
            T[("x2", ig)] = (dx2, em.op("gpsimd", lambda e, ig=ig, p3=ig % 3: e.indirect_dma_start(
                out=x2_sb[:, p3, :], out_offset=None,
                in_=x_in, in_offset=IndirectOffsetOnAxis(gusafe_sb[:, ig:ig + 1], 0)), dx2))

        def emit_qbl(b):
            qb0 = b * NBLK * 128
            qbn = min(NBLK * 128, NTS - qb0)
            if b >= 2 and "ctx" not in SKIP:
                em.wait("sync", pP, T[("ctx", (b - 1) * NBLK - 1)])
            if b >= 1:
                em.wait("sync", dQ, T[("qbl", b - 1)])
            T[("qbl", b)] = em.op("sync", lambda e, qb0=qb0, qbn=qbn, b2=b % 2: e.dma_start(
                out=qblk[:, b2, :, 0:qbn],
                in_=q_dram.rearrange("oc p n -> p oc n")[:, :, qb0:qb0 + qbn]), dQ)

        for ig in range(min(3, NRT)):
            emit_gathers(ig)
        emit_qbl(0)
        for i in range(NRT):
            pb = i % 2
            blk = i // NBLK
            bb = blk % 2
            if i >= 2:
                em.wait("sync", pV, T[("yxn", i - 2)])
            dxn = dXNa if i % 2 == 0 else dXNb
            if i >= 2:
                em.wait("sync", dxn, T[("xnl", i - 2)][1])
            T[("xnl", i)] = (dxn, em.op("sync", lambda e, i=i, pb=pb: e.dma_start(
                out=xn2[:, pb, :], in_=xn_dram[i * 128:(i + 1) * 128, :]), dxn))
            if i % NBLK == 0 and (blk + 1) * NBLK * 128 < NTS + NBLK * 128:
                nb = blk + 1
                if nb * NBLK * 128 < NTS:
                    emit_qbl(nb)

            if i == 0 and "xn20" in DBG:
                em.wait("sync", dX, T[("xnl", 0)])
                em.op("sync", lambda e, d=dbg_out["xn20"]: e.dma_start(out=d, in_=xn2[:, 0, :]), dO)
            if i == 0 and "qb0" in DBG:
                em.wait("sync", dQ, T[("qbl", 0)])
                em.op("sync", lambda e, d=dbg_out["qb0"]: e.dma_start(
                    out=d, in_=qblk[:, 0, 0:1, :].rearrange("p a n -> p (a n)")), dO)
            em.wait("tensor", dQ, T[("qbl", blk)])
            em.wait("tensor", pV, T[("kvblk",)])
            if i >= 1 and ("ctx" not in SKIP):
                em.wait("tensor", pV, T[("yc", i - 1)])
            for g in (range(NG) if "ctx" not in SKIP else []):
                stf = (g % 3 == 0)
                spf = (g % 3 == 2 or g == NG - 1)
                tc = em.op("tensor", lambda e, g=g, i=i, bb=bb, stf=stf, spf=spf: e.matmul(
                    ctx_ps[:, coff(g):coff(g) + 130],
                    qblk[:, bb, g, (i % NBLK) * 128:(i % NBLK) * 128 + 128],
                    kvblk[:, g, :], start=stf, stop=spf), pP)
            if "ctx" not in SKIP:
                T[("ctx", i)] = tc
            if "ffn" not in SKIP:
                em.wait("tensor", T[("wg", i)][0], T[("wg", i)][1])
                em.wait("tensor", dB, T[("bg", i)])
                if i >= 2:
                    em.wait("tensor", pV, T[("ffnadd", i - 2)])
                em.wait("tensor", dG, T[("berbf",)])
                for dc in range(NDC):
                    for hf in range(2):
                        sl = slice(hf * 512, hf * 512 + 512)
                        em.op("tensor", lambda e, dc=dc, sl=sl, i=i, pb=pb, wb=i % 3: e.matmul(
                            ffn_ps[:, pb, sl], xnT[dc][:, i * 128:(i + 1) * 128],
                            wsel[:, wb, dc, sl], start=(dc == 0), stop=False))
                em.wait("tensor", dG, T[("berbf",)])
                for hf in range(2):
                    sl = slice(hf * 512, hf * 512 + 512)
                    em.op("tensor", lambda e, sl=sl, pb=pb, wb=i % 3: e.matmul(
                        ffn_ps[:, pb, sl], ones1b, bsel[0:1, wb, sl],
                        start=False, stop=False))
                for hf in range(2):
                    sl = slice(hf * 512, hf * 512 + 512)
                    tf = em.op("tensor", lambda e, sl=sl, pb=pb: e.matmul(
                        ffn_ps[:, pb, sl], ones1b, ber_bf[:, sl],
                        start=False, stop=True), pP)
                T[("fmm", i)] = tf

            if "ctx" in SKIP:
                T[("yc", i)] = em.op("vector", lambda e: e.memset(y_sb, 0.0), pV)
            else:
                em.wait("vector", pP, T[("ctx", i)])
            if i >= 1:
                em.wait("vector", pV, T[("fin", i - 1)])
            for b3 in (range(3) if "ctx" not in SKIP else []):
                nblk = 6 if b3 < 2 else 4
                dsrc = ctx_ps[:, b3 * 512:b3 * 512 + 65 * nblk].rearrange(
                    "p (n e) -> p n e", e=65)[:, :, 64:65]
                t5 = em.op("vector", lambda e, dsrc=dsrc, b3=b3, nblk=nblk: e.tensor_scalar(
                    dmx_sb[:, 6 * b3:6 * b3 + nblk], dsrc, EPS_DEN, None, ALU.max), pV)
            if "ctx" not in SKIP:
                em.wait("vector", pV, t5)
                t6 = em.op("vector", lambda e: e.reciprocal(dinv_sb, dmx_sb), pV)
                em.wait("vector", pV, t6)
            for b3 in (range(3) if "ctx" not in SKIP else []):
                nblk = 6 if b3 < 2 else 4
                csrc = ctx_ps[:, b3 * 512:b3 * 512 + 65 * nblk].rearrange(
                    "p (n e) -> p n e", e=65)[:, :, 0:64]
                dsl = dinv_sb[:, 6 * b3:6 * b3 + nblk].rearrange(
                    "p n -> p n ()").broadcast_to([128, nblk, 64])
                t7 = em.op("vector", lambda e, csrc=csrc, dsl=dsl, b3=b3, nblk=nblk: e.tensor_tensor(
                    y_sb[:, 384 * b3:384 * b3 + 64 * nblk].rearrange("p (n e) -> p n e", e=64),
                    csrc, dsl, ALU.mult), pV)
            if "ctx" not in SKIP:
                T[("yc", i)] = t7
            em.wait("vector", pV, T[("yc", i)])
            em.wait("vector", T[("xnl", i)][0], T[("xnl", i)][1])
            t8_ = em.op("vector", lambda e, pb=pb: e.tensor_tensor(
                y_sb, y_sb, xn2[:, pb, :], ALU.add), pV)
            T[("yxn", i)] = t8_
            em.wait("scalar", pV, t8_)
            T[("sq2", i)] = em.op("scalar", lambda e: e.activation(
                sq2_scr, y_sb, AF.Square, accum_out=s2(0)), pA)
            em.wait("vector", pV, t8_)
            t = em.op("vector", lambda e: e.reduce_sum(s2(1), y_sb, axis=AX.X), pV)
            em.wait("vector", pV, t)
            t = em.op("vector", lambda e: e.tensor_scalar(s2(2), s2(1), float(RD), None, ALU.mult), pV)
            em.wait("vector", pV, t)
            t = em.op("vector", lambda e: e.tensor_tensor(s2(3), s2(2), s2(2), ALU.mult), pV)
            em.wait("vector", pA, T[("sq2", i)])
            t2 = em.op("vector", lambda e: e.tensor_scalar(s2(4), s2(0), float(RD), None, ALU.mult), pV)
            em.wait("vector", pV, t2)
            t3 = em.op("vector", lambda e: e.tensor_tensor(s2(5), s2(4), s2(3), ALU.subtract), pV)
            em.wait("scalar", pV, t3)
            T[("sd2", i)] = em.op("scalar", lambda e: e.activation(
                s2(6), s2(5), AF.Sqrt, bias=eps_sb), pA)
            em.wait("vector", pA, T[("sd2", i)])
            t4 = em.op("vector", lambda e: e.reciprocal(s2(7), s2(6)), pV)
            em.wait("vector", pV, t4)
            t9 = em.op("vector", lambda e: e.tensor_scalar(
                y_sb, y_sb, s2(2), s2(7), ALU.subtract, ALU.mult), pV)
            em.wait("vector", pV, t9)
            em.wait("vector", pV, T[("gb",)])
            t11 = em.op("vector", lambda e: e.tensor_tensor(y_sb, y_sb, gamma_b, ALU.mult), pV)
            em.wait("vector", pV, t11)
            if "ffn" not in SKIP:
                em.wait("vector", pP, T[("fmm", i)])
                t12 = em.op("vector", lambda e, pb=pb: e.tensor_tensor(
                    y_sb, y_sb, ffn_ps[:, pb, :], ALU.add), pV)
                T[("ffnadd", i)] = t12
                em.wait("vector", pV, t12)
            em.wait("vector", T[("x2", i)][0], T[("x2", i)][1])
            if i >= 2:
                em.wait("vector", dG, T[("ost", i - 2)])
            T[("fin", i)] = em.op("vector", lambda e, pb=pb, p3=i % 3: e.tensor_tensor(
                o_sb[:, pb, :], y_sb, x2_sb[:, p3, :], ALU.add), pV)
            em.wait("gpsimd", pV, T[("fin", i)])
            T[("ost", i)] = em.op("gpsimd", lambda e, i=i, pb=pb: e.indirect_dma_start(
                out=y_out, out_offset=IndirectOffsetOnAxis(gu_sb[:, i:i + 1], 0),
                in_=o_sb[:, pb, :], in_offset=None,
                bounds_check=NTOK - 1, oob_is_err=False), dG)
            if i + 3 < NRT:
                emit_gathers(i + 3)

        em.wait("gpsimd", dG, dG.n)
        em.wait("gpsimd", dGW0, dGW0.n)
        em.wait("gpsimd", dGW1, dGW1.n)
        em.wait("gpsimd", dX2a, dX2a.n)
        em.wait("gpsimd", dX2b, dX2b.n)
        em.wait("gpsimd", dB, dB.n)
        em.wait("gpsimd", dGX, dGX.n)
        em.wait("gpsimd", dWT, dWT.n)
        em.wait("sync", dXNa, dXNa.n)
        em.wait("sync", dXNb, dXNb.n)
        em.wait("scalar", dO, dO.n)
        em.wait("sync", dQ, dQ.n)

        with nc.Block() as block:
            for ename in ENGINES:
                getattr(block, ename)(
                    lambda eng, el=tuple(em.steps[ename]): [fn(eng) for fn in el])

    es.close()
    return nc


def host_prep(Wq, bq, Wk, bk, Wv, bv, gamma, beta, sigs, Wf, bf, NT=32):
    NTOK = NT * 128
    NRT = NT + 7
    def chunks(WT):
        return np.ascontiguousarray(WT.reshape(NDC, 128, D).transpose(1, 0, 2))
    pe = make_pe_np()[:NTOK]
    d = {
        "wkT": chunks(np.ascontiguousarray(Wk.T)),
        "wvT": chunks(np.ascontiguousarray(Wv.T)),
        "wqT": chunks(np.ascontiguousarray(Wq.T)),
        "wfT": np.ascontiguousarray(np.stack(
            [chunks(np.ascontiguousarray(Wf[t].T)) for t in range(8)])),
        "bk_row": np.ascontiguousarray(bk.reshape(1, D)),
        "bv_row": np.ascontiguousarray(bv.reshape(1, D)),
        "bq_col": np.ascontiguousarray(bq.reshape(8, 128).T),
        "g_row": np.ascontiguousarray(gamma.reshape(1, D)),
        "be_row": np.ascontiguousarray(beta.reshape(1, D)),
        "bffn": np.ascontiguousarray(bf),
        "sigsT": np.ascontiguousarray(sigs.T),
        "peJ": np.ascontiguousarray(pe.reshape(128, NT, 32).transpose(2, 1, 0)),
        "ltri": np.asarray(np.concatenate(
            [np.triu(np.ones((128, 128), np.float32), 1),
             np.ones((128, 1), np.float32)], 1), dtype=ml_dtypes.bfloat16),
        "ltri8": np.ascontiguousarray(np.concatenate(
            [np.triu(np.ones((8, 8), np.float32), 1),
             np.ones((8, 1), np.float32)], 1)),
        "iota_pj": np.ascontiguousarray(
            (np.arange(128)[:, None] * NT + np.arange(NT)[None, :]).astype(np.float32)),
        "i128row": np.ascontiguousarray(
            (np.arange(NRT) * 128.0).astype(np.float32).reshape(1, NRT)),
        "thr832": np.ascontiguousarray(np.broadcast_to(
            (np.arange(32) * 128.0).astype(np.float32), (8, 32)).copy()),
        "iota_p8": np.ascontiguousarray(
            (np.arange(128) * 8.0).astype(np.float32).reshape(128, 1)),
        "identf": np.eye(128, dtype=np.float32),
        "identb": np.eye(128, dtype=ml_dtypes.bfloat16),
    }
    return d


def kernel(L0_attn_x, L0_attn_Wq, L0_attn_bq, L0_attn_Wk, L0_attn_bk,
           L0_attn_Wv, L0_attn_bv, L0_attn_gamma, L0_attn_beta,
           L0_ffn_tile_sigs, L0_ffn_W, L0_ffn_b):
    x = np.ascontiguousarray(np.asarray(L0_attn_x, dtype=np.float32))
    prep = host_prep(np.asarray(L0_attn_Wq, np.float32), np.asarray(L0_attn_bq, np.float32),
                     np.asarray(L0_attn_Wk, np.float32), np.asarray(L0_attn_bk, np.float32),
                     np.asarray(L0_attn_Wv, np.float32), np.asarray(L0_attn_bv, np.float32),
                     np.asarray(L0_attn_gamma, np.float32), np.asarray(L0_attn_beta, np.float32),
                     np.asarray(L0_ffn_tile_sigs, np.float32), np.asarray(L0_ffn_W, np.float32),
                     np.asarray(L0_ffn_b, np.float32))
    nc = build_program(NT=32)
    in_maps = []
    for core in range(8):
        m = dict(prep)
        m["x_in"] = np.ascontiguousarray(x[core])
        in_maps.append(m)
    res = run_bass_kernel_spmd(nc, in_maps, core_ids=list(range(8)))
    return np.stack([r["y_out"] for r in res.results], axis=0)
